# version 05
# baseline (speedup 1.0000x reference)
import numpy as np
import ml_dtypes
import concourse.bass as bass
import concourse.mybir as mybir
from concourse.bass_utils import run_bass_kernel_spmd
from contextlib import ExitStack
from collections import deque

F32 = mybir.dt.float32
BF16 = mybir.dt.bfloat16
AF = mybir.ActivationFunctionType
ALU = mybir.AluOpType
AX = mybir.AxisListType

NCORES = 8
B = 2
N = 2048
D = 2048
H = 16
DH = 128
DFF = 8192
DC = D // 128
FCN = DFF // 128
EPS = 1e-5
TANH_SCALE = (DH ** -0.5) / 50.0
NEG = -1.0e9

ENGS = ("sync", "tensor", "vector", "scalar", "gpsimd")


class _CloseLater:
    def __init__(self, st):
        self.st = st

    def __enter__(self):
        return self

    def __exit__(self, *a):
        self.st.close()
        return False

_PHASE_N = [0]


class Phase:
    def __init__(self):
        self.ops = {e: [] for e in ENGS}

    def add(self, eng, fn):
        self.ops[eng].append(fn)

    def flush(self, block):
        import os
        _PHASE_N[0] += 1
        limit = int(os.environ.get("K_UPTO", "1000000"))
        if _PHASE_N[0] > limit:
            self.ops = {e: [] for e in ENGS}
            return
        for e in ENGS:
            fns = self.ops[e]
            if fns:
                getattr(block, e)(lambda en, fns=fns: [f(en) for f in fns])
        self.ops = {e: [] for e in ENGS}


def build(dump=False):
    nc = bass.Bass("TRN2", target_bir_lowering=False)

    x_in = nc.dram_tensor("x", [B, N, D], F32, kind="ExternalInput").ap()
    wqkv = nc.dram_tensor("wqkv", [D, 3 * D], F32, kind="ExternalInput").ap()
    wg_in = nc.dram_tensor("wg", [D, H], F32, kind="ExternalInput").ap()
    wout_in = nc.dram_tensor("wout", [D, D], F32, kind="ExternalInput").ap()
    w1_in = nc.dram_tensor("w1", [DFF, D], F32, kind="ExternalInput").ap()
    b1_in = nc.dram_tensor("b1", [DFF], F32, kind="ExternalInput").ap()
    w2_in = nc.dram_tensor("w2", [D, DFF], F32, kind="ExternalInput").ap()
    b2_in = nc.dram_tensor("b2", [D], F32, kind="ExternalInput").ap()
    out = nc.dram_tensor("out", [B, N, D], F32, kind="ExternalOutput").ap()

    sk = "ExternalOutput" if dump else "Internal"
    xnT_s = nc.dram_tensor("xnT_s", [B, DC, 128, N], BF16, kind=sk).ap()
    qkT_s = nc.dram_tensor("qkT_s", [B, H, 2, 128, N], BF16, kind=sk).ap()
    v_s = nc.dram_tensor("v_s", [B, N, D], BF16, kind=sk).ap()
    gates_s = nc.dram_tensor("gates_s", [H, B * N], BF16, kind=sk).ap()
    s_s = nc.dram_tensor("s_s", [B, N, D], BF16, kind=sk).ap()
    w1qT_s = nc.dram_tensor("w1qT_s", [DC, 128, DFF], BF16, kind=sk).ap()
    w2qT_s = nc.dram_tensor("w2qT_s", [DFF, D], BF16, kind=sk).ap()

    identb_c = nc.inline_tensor(np.eye(128, dtype=ml_dtypes.bfloat16), "identb").ap()
    identf_c = nc.inline_tensor(np.eye(128, dtype=np.float32), "identf").ap()
    ones_row_c = nc.inline_tensor(np.ones((1, 128), dtype=ml_dtypes.bfloat16), "onesrow").ap()
    ones128_c = nc.inline_tensor(np.ones((128, 128), dtype=ml_dtypes.bfloat16), "ones128").ap()
    ones_row_f_c = nc.inline_tensor(np.ones((1, 128), dtype=np.float32), "onesrowf").ap()
    kkn = np.arange(128)[:, None]
    qqn = np.arange(512)[None, :]
    masks_np = np.stack([np.where(kkn + r * 128 > qqn, NEG, 0.0).astype(np.float32)
                         for r in range(4)])
    masks_c = nc.inline_tensor(masks_np, "masks").ap()
    taubias_c = nc.inline_tensor(np.array([[-1e-8, 0.5e-8, -0.5e-8]], dtype=np.float32),
                                 "taubias").ap()

    sems = {}
    cnt = {}

    def inc(name, k=1):
        cnt[name] += k
        return cnt[name]

    with ExitStack() as stack:
        ee = stack.enter_context

        for name in ("bar", "c_ld", "c_ld2",
                     "x_ld", "ln_st", "ln_bn", "ln_norm", "ln_tp", "ln_ev", "xnT_st",
                     "w_ld", "qk_mm", "qk_ev", "qk_st", "g_mm",
                     "v_mm", "v_ev", "v_st",
                     "at_ld", "at_sim", "at_tanh", "at_mask", "at_exp",
                     "at_av", "at_rec", "at_t1", "at_t2", "at_bc", "at_bcev",
                     "s4_mm", "s4_ev", "s4_st",
                     "t5_ld", "t5_abs", "t5_tau", "t5_tern_v", "t5_tern_g",
                     "t5_tp", "t5_evs", "t5_evv", "t5_st",
                     "f6_xnb", "f6_w1", "f6_w2", "f6_h", "f6_gelu",
                     "f6_y", "f6_yev", "f6_silu", "f6_sld", "f6_xld",
                     "f6_cmb", "f6_out"):
            sems[name] = ee(nc.semaphore(name))
            cnt[name] = 0

        def barrier(ph):
            tgt = cnt["bar"] + 5
            cnt["bar"] = tgt
            hh = sems["bar"]
            for e in ENGS:
                ph.add(e, lambda en, t=tgt: (en.sem_inc(hh, 1), en.wait_ge(hh, t)))

        identb = ee(nc.sbuf_tensor("identb_b", [128, 128], BF16))
        identf = ee(nc.sbuf_tensor("identf_b", [128, 128], F32))
        ones_row = ee(nc.sbuf_tensor("ones_row_b", [1, 128], BF16))
        ones128 = ee(nc.sbuf_tensor("ones128_b", [128, 128], BF16))
        ones_row_f = ee(nc.sbuf_tensor("ones_row_f_b", [1, 128], F32))
        masks = ee(nc.sbuf_tensor("masks_b", [128, 4 * 512], F32))
        b1_sb = ee(nc.sbuf_tensor("b1_sb", [128, FCN], F32))
        b2row = ee(nc.sbuf_tensor("b2row_b", [1, D], F32))
        taub = ee(nc.sbuf_tensor("taub_b", [1, 3], F32))
        tau128 = ee(nc.sbuf_tensor("tau128_b", [128, 4], F32))

        block = ee(nc.Block())

        ph = Phase()

        def _setup(sy):
            for dst, src in ((identb[:], identb_c), (identf[:], identf_c),
                             (ones_row[:], ones_row_c), (ones128[:], ones128_c),
                             (ones_row_f[:], ones_row_f_c),
                             (b2row[:], b2_in.rearrange("(one d) -> one d", one=1)),
                             (taub[:], taubias_c)):
                sy.dma_start(out=dst, in_=src).then_inc(sems["c_ld"], 16)
            sy.dma_start(out=masks[:].rearrange("p (r q) -> p r q", r=4),
                         in_=masks_c.rearrange("r p q -> p r q")).then_inc(sems["c_ld"], 16)
        ph.add("sync", _setup)
        inc("c_ld", 8 * 16)

        def _setup_gp(gp):
            gp.dma_start(out=b1_sb[:], in_=b1_in.rearrange("(fc p) -> p fc", p=128),
                         allow_slow_non_contiguous=True).then_inc(sems["c_ld2"], 16)
        ph.add("gpsimd", _setup_gp)
        inc("c_ld2", 16)
        barrier(ph)
        ph.flush(block)

        passA = {"i": 0}

        def _abssrc(i):
            if i < 64:
                return w1_in[i * 128:(i + 1) * 128, :]
            j = i - 64
            dc, c = j // 4, j % 4
            return w2_in[dc * 128:(dc + 1) * 128, c * D:(c + 1) * D]

        def passA_emit(ph, ntiles, bufs):
            wta, absp = bufs["wta"], bufs["absp"]
            for _ in range(ntiles):
                i = passA["i"]
                if i >= 128:
                    return
                passA["i"] = i + 1

                def _ld(sy, i=i, tw=cnt["t5_abs"] - 3 + (i - passA["i"] + 1)):
                    if i >= 4:
                        sy.wait_ge(sems["t5_abs"], i - 3)
                    sy.dma_start(out=wta[i % 4][:], in_=_abssrc(i)
                                 ).then_inc(sems["t5_ld"], 16)
                ph.add("sync", _ld)
                ld_v = inc("t5_ld", 16)

                def _red(ve, i=i, wv=ld_v):
                    ve.wait_ge(sems["t5_ld"], wv)
                    ve.tensor_reduce(absp[:, i:i + 1], wta[i % 4][:], AX.X,
                                     ALU.add, apply_absolute_value=True
                                     ).then_inc(sems["t5_abs"], 1)
                    ve.wait_ge(sems["t5_abs"], i + 1)
                ph.add("vector", _red)
                inc("t5_abs", 1)

        def passA_final(ph, bufs):
            absp, redv = bufs["absp"], bufs["redv"]

            def _fin(ve, w0=cnt["t5_abs"]):
                ve.wait_ge(sems["t5_abs"], w0)
                ve.tensor_reduce(redv[:, 0:1], absp[:, 0:64], AX.X, ALU.add
                                 ).then_inc(sems["t5_abs"], 1)
                ve.wait_ge(sems["t5_abs"], w0 + 1)
                ve.tensor_reduce(redv[:, 1:2], absp[:, 64:128], AX.X, ALU.add
                                 ).then_inc(sems["t5_abs"], 1)
                ve.wait_ge(sems["t5_abs"], w0 + 2)
            ph.add("vector", _fin)
            inc("t5_abs", 2)

        tern_masked_exp_after = []

        for b in range(B):
            tern = (b == 0)
            with ExitStack() as sca:
                ea = sca.enter_context
                xnT_sb = ea(nc.sbuf_tensor(f"xnT_sb{b}", [128, DC * N], BF16))
                wvh = ea(nc.sbuf_tensor(f"wvh{b}", [128, DC * 1024], BF16))
                wq4 = [ea(nc.sbuf_tensor(f"wq4_{b}_{i}", [128, DC * 256], BF16))
                       for i in range(2)]
                xf = [ea(nc.sbuf_tensor(f"xf{b}_{i}", [128, D], F32)) for i in range(2)]
                import os as _os0
                xn_t = [ea(nc.sbuf_tensor(
                    f"xnt{b}_0", [128, D],
                    F32 if _os0.environ.get("K_OLDP0", "1") == "1" else BF16))]
                bnst = ea(nc.sbuf_tensor(f"bnst{b}", [128, 24], F32))
                bnmv = ea(nc.sbuf_tensor(f"bnmv{b}", [128, 2], F32))
                veps = ea(nc.sbuf_tensor(f"veps{b}", [128, 1], F32))
                sdv = ea(nc.sbuf_tensor(f"sdv{b}", [128, 1], F32))
                rinv = ea(nc.sbuf_tensor(f"rinv{b}", [128, 1], F32))
                stage = [ea(nc.sbuf_tensor(f"stageA{b}_{i}", [128, D], BF16))
                         for i in range(2)]
                gstage = ea(nc.sbuf_tensor(f"gstage{b}", [16, N], BF16))
                wgsb = ea(nc.sbuf_tensor(f"wgsb{b}", [128, DC * 16], BF16))
                if tern:
                    pa_bufs = {
                        "wta": [ea(nc.sbuf_tensor(f"wta{i}", [128, D], F32))
                                for i in range(4)],
                        "absp": ea(nc.sbuf_tensor("absp", [128, 128], F32)),
                        "redv": ea(nc.sbuf_tensor("redv", [128, 2], F32)),
                        "onesf": ea(nc.sbuf_tensor("onesf", [128, 1], F32)),
                        "tau11": ea(nc.sbuf_tensor("tau11", [1, 4], F32)),
                        "tauA": ea(nc.sbuf_tensor("tauA", [1, 2], F32)),
                    }

                xnT_view = xnT_sb[:].rearrange("p (dc t) -> p dc t", dc=DC)

                import os as _os
                _nopa = _os.environ.get("K_NOPA", "0") == "1"
                _oldp0 = _os.environ.get("K_OLDP0", "1") == "1"
                ps0 = ExitStack()
                if _oldp0:
                    psA = ps0.enter_context(nc.psum_tensor(f"psA{b}", [128, 2048], F32))
                    psB = ps0.enter_context(nc.psum_tensor(f"psB{b}", [128, 2048], F32))
                    psLn = None
                else:
                    psLn = ps0.enter_context(nc.psum_tensor(f"psLn{b}", [128, 1024], BF16))
                ph = Phase()
                if tern:
                    def _ones(ve):
                        ve.memset(pa_bufs["onesf"][:], 1.0)
                    ph.add("vector", _ones)
                for tt in range(16):
                    sl = tt % 2
                    b_norm = cnt["ln_norm"]

                    def _ld(sy, b=b, tt=tt, sl=sl, nw=b_norm - 1):
                        if tt >= 2:
                            sy.wait_ge(sems["ln_norm"], nw)
                        sy.dma_start(out=xf[sl][:],
                                     in_=x_in[b, tt * 128:(tt + 1) * 128, :]
                                     ).then_inc(sems["x_ld"], 16)
                    ph.add("sync", _ld)
                    x_ld_v = inc("x_ld", 16)

                    bnb = cnt["ln_bn"]

                    def _ln(ve, sl=sl, wv=x_ld_v, bnb=bnb):
                        ve.wait_ge(sems["x_ld"], wv)
                        k = bnb
                        for c in range(4):
                            ve.bn_stats(bnst[:, c * 6:(c + 1) * 6],
                                        xf[sl][:, c * 512:(c + 1) * 512]
                                        ).then_inc(sems["ln_bn"], 1)
                            k += 1
                            ve.wait_ge(sems["ln_bn"], k)
                        ve.bn_aggr(bnmv[:], bnst[:]).then_inc(sems["ln_bn"], 1)
                        k += 1
                        ve.wait_ge(sems["ln_bn"], k)
                        ve.tensor_scalar_add(veps[:], bnmv[:, 1:2], EPS
                                             ).then_inc(sems["ln_st"], 1)
                    ph.add("vector", _ln)
                    inc("ln_bn", 5)
                    v1 = inc("ln_st", 1)

                    def _sq(sc, wv=v1):
                        sc.wait_ge(sems["ln_st"], wv)
                        sc.sqrt(sdv[:], veps[:]).then_inc(sems["ln_st"], 1)
                    ph.add("scalar", _sq)
                    v2 = inc("ln_st", 1)

                    def _nrm(ve, sl=sl, wv=v2, tpw=cnt["ln_tp"], tt=tt):
                        ve.wait_ge(sems["ln_st"], wv)
                        ve.reciprocal(rinv[:], sdv[:])
                        if tt >= 1:
                            ve.wait_ge(sems["ln_tp"], tpw)
                        ve.tensor_scalar(xn_t[0][:], xf[sl][:], bnmv[:, 0:1],
                                         rinv[:], ALU.subtract, ALU.mult
                                         ).then_inc(sems["ln_norm"], 1)
                    ph.add("vector", _nrm)
                    norm_v = inc("ln_norm", 1)

                    for g in range(4):
                        gi0 = cnt["ln_tp"]

                        def _tp(te, g=g, wv=norm_v, gi0=gi0):
                            te.wait_ge(sems["ln_norm"], wv)
                            if _oldp0:
                                pbank = (psA if gi0 % 2 == 0 else psB)[:, 0:512]
                                idt = identf
                            else:
                                pbank = psLn[:, (gi0 % 2) * 512:(gi0 % 2 + 1) * 512]
                                idt = identb
                            if gi0 >= 2:
                                te.wait_ge(sems["ln_ev"], gi0 - 1)
                            last = None
                            for j in range(4):
                                dc = g * 4 + j
                                last = te.transpose(pbank[:, j * 128:(j + 1) * 128],
                                                    xn_t[0][:, dc * 128:(dc + 1) * 128],
                                                    idt[:])
                            last.then_inc(sems["ln_tp"], 1)
                        ph.add("tensor", _tp)
                        tp_v = inc("ln_tp", 1)

                        def _ev(sc, g=g, tt=tt, gi0=gi0, wv=tp_v):
                            sc.wait_ge(sems["ln_tp"], wv)
                            if _oldp0:
                                pbank = (psA if gi0 % 2 == 0 else psB)[:, 0:512]
                            else:
                                pbank = psLn[:, (gi0 % 2) * 512:(gi0 % 2 + 1) * 512]
                            dst = xnT_view[:, g * 4:(g + 1) * 4, tt * 128:(tt + 1) * 128]
                            src = pbank.rearrange("p (j t) -> p j t", j=4)
                            sc.copy(dst, src).then_inc(sems["ln_ev"], 1)
                        ph.add("scalar", _ev)
                        inc("ln_ev", 1)

                    if tern and not _nopa:
                        passA_emit(ph, 1, pa_bufs)

                def _xst(sy, b=b, wv=cnt["ln_ev"]):
                    sy.wait_ge(sems["ln_ev"], wv)
                    sy.dma_start(out=xnT_s[b].rearrange("dc p t -> p dc t"),
                                 in_=xnT_view).then_inc(sems["xnT_st"], 16)
                ph.add("sync", _xst)
                inc("xnT_st", 16)
                barrier(ph)
                ph.flush(block)
                if _oldp0:
                    sca.enter_context(_CloseLater(ps0))
                else:
                    ps0.close()
                    psAB = sca.enter_context(ExitStack())
                    psA = psAB.enter_context(nc.psum_tensor(f"psA{b}", [128, 2048], F32))
                    psB = psAB.enter_context(nc.psum_tensor(f"psB{b}", [128, 2048], F32))

                ph = Phase()
                b_qst = cnt["qk_st"]
                for qk in range(2):
                    for qu in range(8):
                        li = qk * 8 + qu

                        def _wld(gp, qk=qk, qu=qu, li=li,
                                 mmw=cnt["qk_mm"] - 2):
                            if li >= 2:
                                gp.wait_ge(sems["qk_mm"], mmw)
                            gp.dma_start(
                                out=wq4[li % 2][:].rearrange("p (dc m) -> p dc m", dc=DC),
                                in_=wqkv[:, qk * D + qu * 256: qk * D + (qu + 1) * 256
                                         ].rearrange("(dc p) m -> p dc m", p=128)
                            ).then_inc(sems["w_ld"], 16)
                        ph.add("gpsimd", _wld)
                        w_ld_v = inc("w_ld", 16)

                        for hs in range(2):
                            h = qu * 2 + hs
                            g_idx = qk * 16 + h

                            def _mm(te, li=li, hs=hs, wv=w_ld_v,
                                    evw=cnt["qk_ev"] - 1, g_idx=g_idx):
                                te.wait_ge(sems["w_ld"], wv)
                                pbank = psA if g_idx % 2 == 0 else psB
                                if g_idx >= 2:
                                    te.wait_ge(sems["qk_ev"], evw)
                                last = None
                                for dc in range(DC):
                                    lhs = wq4[li % 2][:].rearrange(
                                        "p (dc m) -> p dc m", dc=DC)[
                                        :, dc, hs * 128:(hs + 1) * 128]
                                    for tq in range(4):
                                        last = te.matmul(
                                            pbank[:, tq * 512:(tq + 1) * 512], lhs,
                                            xnT_view[:, dc, tq * 512:(tq + 1) * 512],
                                            start=(dc == 0), stop=(dc == DC - 1))
                                last.then_inc(sems["qk_mm"], 1)
                            ph.add("tensor", _mm)
                            mm_v = inc("qk_mm", 1)

                            def _ev(sc, g_idx=g_idx, wv=mm_v,
                                    stw=cnt["qk_st"] - 16):
                                sc.wait_ge(sems["qk_mm"], wv)
                                pbank = psA if g_idx % 2 == 0 else psB
                                if g_idx >= 2:
                                    sc.wait_ge(sems["qk_st"], stw)
                                sc.copy(stage[g_idx % 2][:], pbank[:]
                                        ).then_inc(sems["qk_ev"], 1)
                            ph.add("scalar", _ev)
                            ev_v = inc("qk_ev", 1)

                            def _st(sy, b=b, h=h, qk=qk, g_idx=g_idx, wv=ev_v):
                                sy.wait_ge(sems["qk_ev"], wv)
                                sy.dma_start(out=qkT_s[b, h, qk],
                                             in_=stage[g_idx % 2][:]
                                             ).then_inc(sems["qk_st"], 16)
                            ph.add("sync", _st)
                            inc("qk_st", 16)

                        if tern:
                            passA_emit(ph, 5, pa_bufs)

                def _wgld(gp):
                    gp.dma_start(out=wgsb[:].rearrange("p (dc g) -> p dc g", g=16),
                                 in_=wg_in.rearrange("(dc p) g -> p dc g", p=128)
                                 ).then_inc(sems["w_ld"], 16)
                ph.add("gpsimd", _wgld)
                wg_v = inc("w_ld", 16)

                def _gmm(te, wv=wg_v, evw=cnt["qk_ev"]):
                    te.wait_ge(sems["w_ld"], wv)
                    te.wait_ge(sems["qk_ev"], evw)
                    last = None
                    for tq in range(4):
                        for dc in range(DC):
                            last = te.matmul(
                                psA[0:16, tq * 512:(tq + 1) * 512],
                                wgsb[:].rearrange("p (dc g) -> p dc g", g=16)[:, dc, :],
                                xnT_view[:, dc, tq * 512:(tq + 1) * 512],
                                start=(dc == 0), stop=(dc == DC - 1))
                    last.then_inc(sems["g_mm"], 1)
                ph.add("tensor", _gmm)
                g1 = inc("g_mm", 1)

                def _gsig(sc, wv=g1):
                    sc.wait_ge(sems["g_mm"], wv)
                    sc.activation(gstage[:], psA[0:16, :], AF.Sigmoid
                                  ).then_inc(sems["g_mm"], 1)
                ph.add("scalar", _gsig)
                g2 = inc("g_mm", 1)

                def _gst(sy, b=b, wv=g2):
                    sy.wait_ge(sems["g_mm"], wv)
                    sy.dma_start(out=gates_s[:, b * N:(b + 1) * N],
                                 in_=gstage[:]).then_inc(sems["qk_st"], 16)
                ph.add("sync", _gst)
                inc("qk_st", 16)
                barrier(ph)
                ph.flush(block)

                ph = Phase()
                wvh_view = wvh[:].rearrange("p (dc m) -> p dc m", dc=DC)
                for half in range(2):
                    def _wvld(gp, half=half, mmw=cnt["v_mm"]):
                        if half == 1:
                            gp.wait_ge(sems["v_mm"], mmw)
                        gp.dma_start(
                            out=wvh_view,
                            in_=wqkv[:, 2 * D + half * 1024:2 * D + (half + 1) * 1024
                                     ].rearrange("(dc p) m -> p dc m", p=128)
                        ).then_inc(sems["w_ld"], 16)
                    ph.add("gpsimd", _wvld)
                    wv_v = inc("w_ld", 16)

                    for tt in range(16):
                        i2 = half * 16 + tt

                        def _vmm(te, tt=tt, i2=i2, wv=wv_v, evw=cnt["v_ev"] - 1):
                            te.wait_ge(sems["w_ld"], wv)
                            pbank = psA if i2 % 2 == 0 else psB
                            if i2 >= 2:
                                te.wait_ge(sems["v_ev"], evw)
                            last = None
                            for mj in range(2):
                                for dc in range(DC):
                                    last = te.matmul(
                                        pbank[:, mj * 512:(mj + 1) * 512],
                                        xnT_view[:, dc, tt * 128:(tt + 1) * 128],
                                        wvh_view[:, dc, mj * 512:(mj + 1) * 512],
                                        start=(dc == 0), stop=(dc == DC - 1))
                            last.then_inc(sems["v_mm"], 1)
                        ph.add("tensor", _vmm)
                        vm_v = inc("v_mm", 1)

                        def _vev(sc, i2=i2, wv=vm_v, stw=cnt["v_st"] - 16):
                            sc.wait_ge(sems["v_mm"], wv)
                            pbank = psA if i2 % 2 == 0 else psB
                            if i2 >= 2:
                                sc.wait_ge(sems["v_st"], stw)
                            sc.copy(stage[i2 % 2][:, 0:1024], pbank[:, 0:1024]
                                    ).then_inc(sems["v_ev"], 1)
                        ph.add("scalar", _vev)
                        ve_v = inc("v_ev", 1)

                        def _vst(sy, b=b, tt=tt, half=half, i2=i2, wv=ve_v):
                            sy.wait_ge(sems["v_ev"], wv)
                            sy.dma_start(out=v_s[b, tt * 128:(tt + 1) * 128,
                                                 half * 1024:(half + 1) * 1024],
                                         in_=stage[i2 % 2][:, 0:1024]
                                         ).then_inc(sems["v_st"], 16)
                        ph.add("sync", _vst)
                        inc("v_st", 16)

                        if tern:
                            passA_emit(ph, 1, pa_bufs)

                if tern:
                    passA_final(ph, pa_bufs)
                    redv, onesf = pa_bufs["redv"], pa_bufs["onesf"]
                    tau11, tauA = pa_bufs["tau11"], pa_bufs["tauA"]

                    def _taumm(te, wv=cnt["t5_abs"], evw=cnt["v_ev"]):
                        te.wait_ge(sems["t5_abs"], wv)
                        te.wait_ge(sems["v_ev"], evw)
                        te.matmul(psB[0:1, 0:1], redv[:, 0:1], onesf[:],
                                  start=True, stop=True)
                        te.matmul(psB[0:1, 1:2], redv[:, 1:2], onesf[:], start=True,
                                  stop=True).then_inc(sems["t5_tau"], 1)
                    ph.add("tensor", _taumm)
                    t1v = inc("t5_tau", 1)

                    def _tauv(sc, wv=t1v):
                        sc.wait_ge(sems["t5_tau"], wv)
                        sc.activation(tauA[0:1, :], psB[0:1, 0:2], AF.Relu,
                                      scale=1.0 / (DFF * D), bias=taub[0:1, 0:1])
                        sc.activation(tau11[0:1, 0:2], tauA[0:1, :], AF.Identity,
                                      scale=0.5, bias=taub[0:1, 1:2])
                        sc.activation(tau11[0:1, 2:4], tauA[0:1, :], AF.Identity,
                                      scale=-0.5, bias=taub[0:1, 2:3]
                                      ).then_inc(sems["t5_tau"], 1)
                    ph.add("scalar", _tauv)
                    t2v = inc("t5_tau", 1)

                    def _taubc(te, wv=t2v):
                        te.wait_ge(sems["t5_tau"], wv)
                        te.matmul(psB[:, 0:4], ones_row_f[:], tau11[0:1, 0:4],
                                  start=True, stop=True).then_inc(sems["t5_tau"], 1)
                    ph.add("tensor", _taubc)
                    t3v = inc("t5_tau", 1)

                    def _tauev(sc, wv=t3v):
                        sc.wait_ge(sems["t5_tau"], wv)
                        sc.copy(tau128[:], psB[:, 0:4]).then_inc(sems["t5_tau"], 1)
                    ph.add("scalar", _tauev)
                    inc("t5_tau", 1)
                barrier(ph)
                ph.flush(block)

            import os as _osB
            if _osB.environ.get("K_NOTB", "0") == "1":
                tern = False
            with ExitStack() as scb:
                eb = scb.enter_context
                oaT = eb(nc.sbuf_tensor(f"oaT{b}", [128, H * N], BF16))
                qT = [eb(nc.sbuf_tensor(f"qT{b}_{i}", [128, N], BF16)) for i in range(2)]
                kTb = [eb(nc.sbuf_tensor(f"kT{b}_{i}", [128, N], BF16)) for i in range(2)]
                vvb = [eb(nc.sbuf_tensor(f"vv{b}_{i}", [128, N], BF16)) for i in range(2)]
                grow = [eb(nc.sbuf_tensor(f"grow{b}_{i}", [1, N], BF16)) for i in range(2)]
                ttg = [eb(nc.sbuf_tensor(f"ttg{b}_{i}", [128, 1024], F32)) for i in range(2)]
                tmg = [eb(nc.sbuf_tensor(f"tmg{b}_{i}", [128, 1024], F32)) for i in range(2)]
                pTg = [eb(nc.sbuf_tensor(f"pTg{b}_{i}", [128, 1024], BF16)) for i in range(2)]
                bcg = [eb(nc.sbuf_tensor(f"bcg{b}_{i}", [128, 512], F32)) for i in range(2)]
                rrowB = eb(nc.sbuf_tensor(f"rrowB{b}", [128, 512], F32))
                tmpO = eb(nc.sbuf_tensor(f"tmpO{b}", [128, 512], F32))

                ps3 = ExitStack()
                if tern:
                    e3t = ps3.enter_context
                    tb = {
                        "wtile": [e3t(nc.sbuf_tensor(f"twt{i}", [128, 1024], F32))
                                  for i in range(2)],
                        "terna_g": e3t(nc.sbuf_tensor("terna_g", [128, 1024], F32)),
                        "ternb_g": e3t(nc.sbuf_tensor("ternb_g", [128, 1024], F32)),
                        "ternt": [e3t(nc.sbuf_tensor(f"ternt{i}", [128, 1024], BF16))
                                  for i in range(16)],
                        "tstage": [e3t(nc.sbuf_tensor(f"tstage{i}", [128, 1024], BF16))
                                   for i in range(2)],
                    }
                psSt = [ps3.enter_context(nc.psum_tensor(f"psS{b}_{i}", [128, 1024], F32))
                        for i in range(2)]
                psS = [psSt[0][:], psSt[1][:]]
                psO = ps3.enter_context(nc.psum_tensor(f"psO{b}", [128, 512], F32))[:]
                psRB = ps3.enter_context(nc.psum_tensor(f"psRB{b}", [128, 512], F32))[:]
                psBc = ps3.enter_context(nc.psum_tensor(f"psBc{b}", [128, 512], F32))[:]
                psTt = ps3.enter_context(nc.psum_tensor(f"psTt{b}", [128, 512], BF16))
                psT2 = [psTt[:], psTt[:]]

                cnt_t5ld_base = [cnt["t5_ld"]]
                import os as _osT
                _tblv = {"ld": 1, "cmp": 2, "tp": 3, "ev": 4, "st": 5,
                         "full": 9}[_osT.environ.get("K_TB", "full")]

                def _tern_done(en, i, off=1):
                    en.wait_ge(sems["t5_tern_g"], i + off)

                def tern_load(sy, i):
                    mi, fc, half = i // 128, (i % 128) // 2, i % 2
                    if i >= 2 and _tblv >= 2:
                        _tern_done(sy, i - 2)
                    if mi == 0:
                        sy.dma_start(out=tb["wtile"][i % 2][:],
                                     in_=w1_in[fc * 128:(fc + 1) * 128,
                                               half * 1024:(half + 1) * 1024]
                                     ).then_inc(sems["t5_ld"], 16)
                    else:
                        sy.dma_start(
                            out=tb["wtile"][i % 2][:].rearrange(
                                "p (dc f) -> p dc f", dc=8),
                            in_=w2_in[half * 1024:(half + 1) * 1024,
                                      fc * 128:(fc + 1) * 128
                                      ].rearrange("(dc p) f -> p dc f", p=128)
                        ).then_inc(sems["t5_ld"], 16)

                def tern_cmp(en, i, first, ta):
                    if _tblv < 2:
                        return
                    mi = i // 128
                    en.wait_ge(sems["t5_ld"], 16 * (i + 1) + cnt_t5ld_base[0])
                    if first:
                        en.wait_ge(sems["t5_tau"], 4)
                    if i >= 16 and _tblv >= 3:
                        en.wait_ge(sems["t5_tp"], 2 * (i - 16) + 2)
                    en.tensor_scalar(ta[0][:], tb["wtile"][i % 2][:],
                                     tau128[:, mi:mi + 1], 1.0,
                                     ALU.is_gt, ALU.subtract)
                    en.tensor_scalar(ta[1][:], tb["wtile"][i % 2][:],
                                     tau128[:, 2 + mi:3 + mi], None, ALU.is_ge)
                    en.tensor_tensor(tb["ternt"][i % 16][:], ta[0][:], ta[1][:],
                                     ALU.add).then_inc(sems["t5_tern_g"], 1)

                def tern_tp(te, i, g2, G):
                    if _tblv < 3:
                        return
                    _tern_done(te, i)
                    if _tblv >= 4:
                        if g2 == 0:
                            if i >= 1:
                                te.wait_ge(sems["t5_evv"], i)
                        else:
                            te.wait_ge(sems["t5_evs"], i + 1)
                    bank = psT2[G % 2]
                    last = None
                    for j in range(4):
                        dcs = g2 * 4 + j
                        last = te.transpose(bank[:, j * 128:(j + 1) * 128],
                                            tb["ternt"][i % 16][:, dcs * 128:(dcs + 1) * 128],
                                            identb[:])
                    last.then_inc(sems["t5_tp"], 1)

                def tern_evac(en, i, g2, G):
                    if _tblv < 4:
                        return
                    en.wait_ge(sems["t5_tp"], G + 1)
                    if g2 == 0 and i >= 2 and _tblv >= 5:
                        en.wait_ge(sems["t5_st"], 16 * (i - 1))
                    dst = tb["tstage"][i % 2][:].rearrange(
                        "p (g2 j f) -> p g2 j f", g2=2, j=4)[:, g2]
                    src = psT2[G % 2].rearrange("p (j f) -> p j f", j=4)
                    sem = sems["t5_evs"] if g2 == 0 else sems["t5_evv"]
                    en.copy(dst, src).then_inc(sem, 1)

                def tern_store(sy, i):
                    if _tblv < 5:
                        return
                    mi, fc, half = i // 128, (i % 128) // 2, i % 2
                    sy.wait_ge(sems["t5_evs"], i + 1)
                    sy.wait_ge(sems["t5_evv"], i + 1)
                    if mi == 0:
                        dst = w1qT_s[half * 8:(half + 1) * 8, :,
                                     fc * 128:(fc + 1) * 128
                                     ].rearrange("dc p f -> p dc f")
                        sy.dma_start(out=dst, in_=tb["tstage"][i % 2][:].rearrange(
                            "p (dc f) -> p dc f", dc=8)).then_inc(sems["t5_st"], 16)
                    else:
                        sy.dma_start(out=w2qT_s[fc * 128:(fc + 1) * 128,
                                                half * 1024:(half + 1) * 1024],
                                     in_=tb["tstage"][i % 2][:]
                                     ).then_inc(sems["t5_st"], 16)

                ph = Phase()
                v_st_total = cnt["v_st"]
                qk_st_total = cnt["qk_st"]
                for h in range(H):
                    sl = h % 2
                    Hg = b * H + h

                    def _atld(sc, b=b, h=h, sl=sl, Hg=Hg, otw=cnt["at_t2"] - 8,
                              first=(h == 0), vw=v_st_total, qw=qk_st_total):
                        if first:
                            sc.wait_ge(sems["v_st"], vw)
                            sc.wait_ge(sems["qk_st"], qw)
                        if Hg >= 2:
                            sc.wait_ge(sems["at_t2"], otw)
                        sc.dma_start(out=qT[sl][:], in_=qkT_s[b, h, 0]
                                     ).then_inc(sems["at_ld"], 16)
                        sc.dma_start(out=kTb[sl][:], in_=qkT_s[b, h, 1]
                                     ).then_inc(sems["at_ld"], 16)
                        sc.dma_start(out=vvb[sl][:].rearrange("p (kt j) -> p kt j", kt=16),
                                     in_=v_s[b, :, h * 128:(h + 1) * 128
                                             ].rearrange("(kt p) j -> p kt j", p=128)
                                     ).then_inc(sems["at_ld"], 16)
                        sc.dma_start(out=grow[sl][:],
                                     in_=gates_s[h:h + 1, b * N:(b + 1) * N]
                                     ).then_inc(sems["at_ld"], 16)
                    ph.add("scalar", _atld)
                    at_ld_v = inc("at_ld", 64)

                    if tern:
                        c_lo = h * 16
                        pe_sprinkle = deque()
                        if h >= 1:
                            p_lo = (h - 1) * 16
                            for ii in range(p_lo, p_lo + 16):
                                for g2 in range(2):
                                    G = 2 * ii + g2
                                    pe_sprinkle.append(
                                        lambda te, ii=ii, g2=g2, G=G:
                                        tern_tp(te, ii, g2, G))
                            inc("t5_tp", 32)
                            evac_sched = [(ii, g2, 2 * ii + g2)
                                          for ii in range(p_lo, p_lo + 16)
                                          for g2 in range(2)]
                            inc("t5_evs", 16)
                            inc("t5_evv", 16)
                            store_sched = list(range(p_lo, p_lo + 16))
                            inc("t5_st", 16 * 16)
                        else:
                            evac_sched = []
                            store_sched = []
                        load_sched = list(range(c_lo, c_lo + 16))
                        inc("t5_ld", 16 * 16)
                        cmp_all = list(load_sched)
                        inc("t5_tern_g", 16)

                        def _tload(sy, load_sched=load_sched,
                                   store_sched=store_sched):
                            li = list(load_sched)
                            si = list(store_sched)
                            while li or si:
                                for _ in range(2):
                                    if li:
                                        tern_load(sy, li.pop(0))
                                for _ in range(2):
                                    if si:
                                        tern_store(sy, si.pop(0))
                        ph.add("sync", _tload)

                        def _tcmp_g(gp, lst=cmp_all, h=h):
                            for i in lst:
                                tern_cmp(gp, i, (h == 0 and i == lst[0]),
                                         (tb["terna_g"], tb["ternb_g"]))
                        ph.add("gpsimd", _tcmp_g)

                        evac_q = [evac_sched[8 * q:8 * q + 8] for q in range(4)]
                    else:
                        pe_sprinkle = deque()
                        evac_q = [[] for _ in range(4)]

                    for qb in range(4):
                        nkt = 4 * (qb + 1)
                        ngrp = nkt // 2
                        Q = cnt["at_bc"]
                        sim0 = cnt["at_sim"]
                        exp0 = cnt["at_exp"]
                        av0 = cnt["at_av"]
                        t1_prev = cnt["at_t1"]
                        rec_prev = cnt["at_rec"]
                        bcev_prev = cnt["at_bcev"]

                        def _pe(te, sl=sl, qb=qb, nkt=nkt, ngrp=ngrp,
                                ldv=at_ld_v, sim0=sim0, exp0=exp0,
                                Q=Q, t1_prev=t1_prev, rec_prev=rec_prev,
                                bcev_prev=bcev_prev,
                                sprinkle=(list(pe_sprinkle)[8 * qb:8 * qb + 8]
                                          if tern else [])):
                            sp = list(sprinkle)

                            def pop_sp():
                                if sp:
                                    sp.pop(0)(te)

                            te.wait_ge(sems["at_ld"], ldv)
                            if Q >= 1:
                                te.wait_ge(sems["at_bcev"], bcev_prev)
                            te.matmul(psBc, ones_row[:],
                                      grow[sl][0:1, qb * 512:(qb + 1) * 512],
                                      start=True, stop=True).then_inc(sems["at_bc"], 1)

                            def av_rs(g):
                                j0, j1 = 2 * g, 2 * g + 1
                                pt = pTg[(exp0 + g) % 2]
                                te.wait_ge(sems["at_exp"], exp0 + g + 1)
                                if g == 0 and Q >= 1:
                                    te.wait_ge(sems["at_t1"], t1_prev)
                                    te.wait_ge(sems["at_rec"], rec_prev)
                                te.matmul(psO, vvb[sl][:, j0 * 128:(j0 + 1) * 128],
                                          pt[:, 0:512], start=(j0 == 0), stop=False)
                                te.matmul(psO, vvb[sl][:, j1 * 128:(j1 + 1) * 128],
                                          pt[:, 512:1024], start=False,
                                          stop=(j1 == nkt - 1))
                                pop_sp()
                                te.matmul(psRB, ones128[:], pt[:, 0:512],
                                          start=(j0 == 0), stop=False)
                                te.matmul(psRB, ones128[:], pt[:, 512:1024],
                                          start=False, stop=(j1 == nkt - 1)
                                          ).then_inc(sems["at_av"], 1)
                                pop_sp()

                            for g in range(ngrp):
                                gsi = sim0 + g
                                if gsi >= 2:
                                    te.wait_ge(sems["at_tanh"], gsi - 1)
                                te.matmul(psS[gsi % 2][:, 0:512],
                                          kTb[sl][:, (2 * g) * 128:(2 * g + 1) * 128],
                                          qT[sl][:, qb * 512:(qb + 1) * 512],
                                          start=True, stop=True)
                                te.matmul(psS[gsi % 2][:, 512:1024],
                                          kTb[sl][:, (2 * g + 1) * 128:(2 * g + 2) * 128],
                                          qT[sl][:, qb * 512:(qb + 1) * 512],
                                          start=True, stop=True
                                          ).then_inc(sems["at_sim"], 1)
                                if g >= 1:
                                    av_rs(g - 1)
                            av_rs(ngrp - 1)
                            while sp:
                                sp.pop(0)(te)
                        ph.add("tensor", _pe)
                        inc("at_bc", 1)
                        for g in range(ngrp):
                            inc("at_sim", 1)
                            inc("at_av", 1)

                        sc_evs = evac_q[qb]

                        for g in range(ngrp):
                            rel0 = 2 * g - 4 * qb
                            masked = rel0 >= 0
                            gti = sim0 + g
                            gei = exp0 + g

                            def _tanh(sc, g=g, sv=sim0 + g + 1, gti=gti,
                                      expw=gti - 1):
                                sc.wait_ge(sems["at_sim"], sv)
                                if gti >= 2:
                                    sc.wait_ge(sems["at_exp"], expw)
                                sc.activation(ttg[gti % 2][:], psS[gti % 2],
                                              AF.Tanh, scale=TANH_SCALE
                                              ).then_inc(sems["at_tanh"], 1)
                            ph.add("scalar", _tanh)
                            tanh_v = inc("at_tanh", 1)

                            if masked:
                                mi = cnt["at_mask"]
                                slot_w = (tern_masked_exp_after[mi - 2]
                                          if mi >= 2 else 0)
                                tern_masked_exp_after.append(gei + 1)

                                def _mask(ve, rel0=rel0, wv=tanh_v, mi=mi,
                                          gti=gti, slot_w=slot_w):
                                    ve.wait_ge(sems["at_tanh"], wv)
                                    if slot_w >= 1:
                                        ve.wait_ge(sems["at_exp"], slot_w)
                                    t1 = None
                                    for s in range(2):
                                        rel = rel0 + s
                                        w = (rel + 1) * 128
                                        t1 = ve.tensor_tensor(
                                            tmg[mi % 2][:, s * 512:s * 512 + w],
                                            ttg[gti % 2][:, s * 512:s * 512 + w],
                                            masks[:, rel * 512:rel * 512 + w],
                                            ALU.add)
                                        if w < 512:
                                            t1 = ve.tensor_copy(
                                                tmg[mi % 2][:, s * 512 + w:(s + 1) * 512],
                                                ttg[gti % 2][:, s * 512 + w:(s + 1) * 512])
                                    t1.then_inc(sems["at_mask"], 1)
                                ph.add("vector", _mask)
                                mask_v = inc("at_mask", 1)
                            else:
                                mask_v = None

                            def _exp(sc, masked=masked, mask_v=mask_v,
                                     gti=gti, gei=gei,
                                     mi=(cnt["at_mask"] - 1 if masked else 0),
                                     evp=sc_evs[2 * g:2 * g + 2]):
                                if masked:
                                    sc.wait_ge(sems["at_mask"], mask_v)
                                if gei >= 2:
                                    sc.wait_ge(sems["at_av"], gei - 1)
                                src = tmg[mi % 2][:] if masked else ttg[gti % 2][:]
                                sc.activation(pTg[gei % 2][:], src, AF.Exp,
                                              scale=50.0).then_inc(sems["at_exp"], 1)
                                for ev in evp:
                                    tern_evac(sc, *ev)
                            ph.add("scalar", _exp)
                            inc("at_exp", 1)

                        if len(sc_evs) > 2 * ngrp:
                            rest = sc_evs[2 * ngrp:]

                            def _scev(sc, rest=rest):
                                for ev in rest:
                                    tern_evac(sc, *ev)
                            ph.add("scalar", _scev)

                        def _bcev(ve, Q=Q, bcw=cnt["at_bc"], t2w=cnt["at_t2"] - 1):
                            ve.wait_ge(sems["at_bc"], bcw)
                            if Q >= 2:
                                ve.wait_ge(sems["at_t2"], t2w)
                            ve.tensor_copy(bcg[Q % 2][:], psBc
                                           ).then_inc(sems["at_bcev"], 1)
                        ph.add("vector", _bcev)
                        inc("at_bcev", 1)

                        av_full = cnt["at_av"]

                        def _tail(ve, h=h, qb=qb, Q=Q, wv=av_full):
                            ve.wait_ge(sems["at_av"], wv)
                            ve.reciprocal(rrowB[:], psRB).then_inc(sems["at_rec"], 1)
                            ve.tensor_tensor(tmpO[:], psO, rrowB[:], ALU.mult
                                             ).then_inc(sems["at_t1"], 1)
                            ve.tensor_tensor(
                                oaT[:, h * N + qb * 512:h * N + (qb + 1) * 512],
                                tmpO[:], bcg[Q % 2][:], ALU.mult
                                ).then_inc(sems["at_t2"], 1)
                        ph.add("vector", _tail)
                        inc("at_rec", 1)
                        inc("at_t1", 1)
                        inc("at_t2", 1)

                if tern:
                    p_lo = 15 * 16

                    def _ttail_pe(te, p_lo=p_lo):
                        for ii in range(p_lo, p_lo + 16):
                            for g2 in range(2):
                                G = 2 * ii + g2
                                tern_tp(te, ii, g2, G)
                    ph.add("tensor", _ttail_pe)
                    inc("t5_tp", 32)

                    def _ttail_evs(sc, p_lo=p_lo):
                        for ii in range(p_lo, p_lo + 16):
                            tern_evac(sc, ii, 0, 2 * ii)
                            tern_evac(sc, ii, 1, 2 * ii + 1)
                    ph.add("scalar", _ttail_evs)
                    inc("t5_evs", 16)
                    inc("t5_evv", 16)

                    def _ttail_st(sy, p_lo=p_lo):
                        for ii in range(p_lo, p_lo + 16):
                            tern_store(sy, ii)
                    ph.add("sync", _ttail_st)
                    inc("t5_st", 16 * 16)

                barrier(ph)
                ph.flush(block)
                ps3.close()

                with ExitStack() as scc:
                    ec = scc.enter_context
                    woutd = [ec(nc.sbuf_tensor(f"woutd{b}_{i}", [128, H * 512], BF16))
                             for i in range(2)]
                    stageB = [ec(nc.sbuf_tensor(f"stageB{b}_{i}", [128, 512], BF16))
                              for i in range(2)]
                    psP4 = ec(nc.psum_tensor(f"psP4_{b}", [128, 2048], F32))
                    ph = Phase()
                    b_smm = cnt["s4_mm"]
                    b_sev = cnt["s4_ev"]
                    b_sst = cnt["s4_st"]
                    for dj in range(4):
                        def _wodld(gp, b=b, dj=dj, mmw=b_smm + 16 * (dj - 1)):
                            if dj >= 2:
                                gp.wait_ge(sems["s4_mm"], mmw)
                            gp.dma_start(
                                out=woutd[dj % 2][:].rearrange("p (hc m) -> p hc m", hc=H),
                                in_=wout_in[:, dj * 512:(dj + 1) * 512
                                            ].rearrange("(hc p) m -> p hc m", p=128)
                            ).then_inc(sems["w_ld"], 16)
                        ph.add("gpsimd", _wodld)
                        wod_v = inc("w_ld", 16)

                        for tt in range(16):
                            i4 = dj * 16 + tt

                            def _smm(te, dj=dj, tt=tt, i4=i4, wv=wod_v,
                                     evw=b_sev + i4 - 3):
                                te.wait_ge(sems["w_ld"], wv)
                                if i4 >= 4:
                                    te.wait_ge(sems["s4_ev"], evw)
                                pb = psP4[:, (i4 % 4) * 512:(i4 % 4 + 1) * 512]
                                last = None
                                for hc in range(H):
                                    last = te.matmul(
                                        pb,
                                        oaT[:, hc * N + tt * 128:hc * N + (tt + 1) * 128],
                                        woutd[dj % 2][:].rearrange(
                                            "p (hc m) -> p hc m", hc=H)[:, hc, :],
                                        start=(hc == 0), stop=(hc == H - 1))
                                last.then_inc(sems["s4_mm"], 1)
                            ph.add("tensor", _smm)
                            sm_v = inc("s4_mm", 1)

                            def _sev(sc, i4=i4, wv=sm_v, stw=b_sst + 16 * (i4 - 1)):
                                sc.wait_ge(sems["s4_mm"], wv)
                                if i4 >= 2:
                                    sc.wait_ge(sems["s4_st"], stw)
                                pb = psP4[:, (i4 % 4) * 512:(i4 % 4 + 1) * 512]
                                sc.copy(stageB[i4 % 2][:], pb).then_inc(sems["s4_ev"], 1)
                            ph.add("scalar", _sev)
                            se_v = inc("s4_ev", 1)

                            def _sst(sy, b=b, dj=dj, tt=tt, i4=i4, wv=se_v):
                                sy.wait_ge(sems["s4_ev"], wv)
                                sy.dma_start(out=s_s[b, tt * 128:(tt + 1) * 128,
                                                     dj * 512:(dj + 1) * 512],
                                             in_=stageB[i4 % 2][:]).then_inc(sems["s4_st"], 16)
                            ph.add("sync", _sst)
                            inc("s4_st", 16)
                    barrier(ph)
                    ph.flush(block)

        s4_after_b = cnt["s4_st"]

        with ExitStack() as sc3:
            e3 = sc3.enter_context
            xnb = e3(nc.sbuf_tensor("xnb", [128, DC * 1024], BF16))
            w1sl = [e3(nc.sbuf_tensor(f"w1sl{i}", [128, DC * 256], BF16))
                    for i in range(2)]
            w2gb = e3(nc.sbuf_tensor("w2gb", [128, 8 * D], BF16))
            hTg = e3(nc.sbuf_tensor("hTg", [128, 8 * 1024], BF16))
            y_acc = e3(nc.sbuf_tensor("y_acc", [128, 8 * D], BF16))
            b2b = e3(nc.sbuf_tensor("b2b", [128, D], BF16))
            yb = e3(nc.sbuf_tensor("yb", [128, D], F32))
            gg = e3(nc.sbuf_tensor("gg", [128, D], BF16))
            sld = [e3(nc.sbuf_tensor(f"sld{i}", [128, D], BF16)) for i in range(2)]
            xld = [e3(nc.sbuf_tensor(f"xld{i}", [128, D], F32)) for i in range(2)]
            ost = [e3(nc.sbuf_tensor(f"ost{i}", [128, D], F32)) for i in range(2)]
            psH = [e3(nc.psum_tensor(f"psH{i}", [128, 1024], F32)) for i in range(2)]
            psY = [e3(nc.psum_tensor(f"psY{i}", [128, 512], F32)) for i in range(2)]

            xnb_view = xnb[:].rearrange("p (dc t) -> p dc t", dc=DC)

            ph = Phase()

            def _b2mm(te):
                for dj in range(4):
                    if dj >= 2:
                        te.wait_ge(sems["f6_yev"], dj - 1)
                    te.matmul(psY[dj % 2][:], ones_row_f[:],
                              b2row[0:1, dj * 512:(dj + 1) * 512],
                              start=True, stop=True).then_inc(sems["f6_y"], 1)
            ph.add("tensor", _b2mm)

            def _b2ev(sc):
                for dj in range(4):
                    sc.wait_ge(sems["f6_y"], dj + 1)
                    sc.copy(b2b[:, dj * 512:(dj + 1) * 512], psY[dj % 2][:]
                            ).then_inc(sems["f6_yev"], 1)
            ph.add("scalar", _b2ev)
            for dj in range(4):
                inc("f6_y", 1)
                inc("f6_yev", 1)
            barrier(ph)
            ph.flush(block)

            NG = 8
            GC = 8
            gsl_ctr = [0]
            t5_st_w1 = 128 * 16
            t5_st_all = 256 * 16
            for blk in range(4):
                b = blk // 2
                t0 = (blk % 2) * 1024
                ph = Phase()
                h_prev_blk = cnt["f6_h"]

                def _xnbld(sy, b=b, t0=t0, hw=h_prev_blk, first=(blk == 0)):
                    if first:
                        sy.wait_ge(sems["xnT_st"], 16)
                    if b == 1 and t0 == 0:
                        sy.wait_ge(sems["xnT_st"], 32)
                    if hw >= 1:
                        sy.wait_ge(sems["f6_h"], hw)
                    sy.dma_start(out=xnb_view,
                                 in_=xnT_s[b][:, :, t0:t0 + 1024
                                              ].rearrange("dc p t -> p dc t")
                                 ).then_inc(sems["f6_xnb"], 16)
                ph.add("sync", _xnbld)
                xnb_v = inc("f6_xnb", 16)

                for g in range(NG):
                    y_prev = cnt["f6_y"]

                    for ss in range(4):
                        gsl = gsl_ctr[0]
                        gsl_ctr[0] += 1
                        sg = g * 4 + ss

                        def _w1ld(sy, sg=sg, gsl=gsl, hw=2 * (gsl - 1)):
                            if gsl == 0:
                                sy.wait_ge(sems["t5_st"], t5_st_w1)
                            if gsl >= 2:
                                sy.wait_ge(sems["f6_h"], hw)
                            sy.dma_start(
                                out=w1sl[gsl % 2][:].rearrange("p (dc f) -> p dc f", dc=DC),
                                in_=w1qT_s[:, :, sg * 256:(sg + 1) * 256
                                           ].rearrange("dc p f -> p dc f")
                            ).then_inc(sems["f6_w1"], 16)
                        ph.add("sync", _w1ld)
                        inc("f6_w1", 16)
                    w1_vals = [cnt["f6_w1"] - 16 * (3 - s) for s in range(4)]

                    def _w2ld(sc, g=g, yw=y_prev, first=(blk == 0 and g == 0)):
                        if first:
                            sc.wait_ge(sems["t5_st"], t5_st_all)
                        if yw >= 1:
                            sc.wait_ge(sems["f6_y"], yw)
                        for j in range(GC):
                            fc = g * GC + j
                            sc.dma_start(out=w2gb[:, j * D:(j + 1) * D],
                                         in_=w2qT_s[fc * 128:(fc + 1) * 128, :]
                                         ).then_inc(sems["f6_w2"], 16)
                    ph.add("scalar", _w2ld)
                    inc("f6_w2", 16 * GC)
                    w2_v = cnt["f6_w2"]

                    gelu0 = cnt["f6_gelu"]
                    for j in range(GC):
                        fc = g * GC + j
                        gh = cnt["f6_h"]

                        def _hmm(te, j=j, gh=gh, xv=xnb_v, yw=y_prev,
                                 wv=w1_vals[j // 2], gsl_base=gsl_ctr[0] - 4,
                                 geluw=gh - 1):
                            te.wait_ge(sems["f6_w1"], wv)
                            if j == 0:
                                te.wait_ge(sems["f6_xnb"], xv)
                                if yw >= 1:
                                    te.wait_ge(sems["f6_y"], yw)
                            if gh >= 2:
                                te.wait_ge(sems["f6_gelu"], geluw)
                            last = None
                            for half in range(2):
                                for dc in range(DC):
                                    last = te.matmul(
                                        psH[gh % 2][:, half * 512:(half + 1) * 512],
                                        w1sl[(gsl_base + j // 2) % 2][:].rearrange(
                                            "p (dc f) -> p dc f", dc=DC)[
                                            :, dc, (j % 2) * 128:(j % 2 + 1) * 128],
                                        xnb_view[:, dc, half * 512:(half + 1) * 512],
                                        start=(dc == 0), stop=(dc == DC - 1))
                            last.then_inc(sems["f6_h"], 1)
                        ph.add("tensor", _hmm)
                        h_v = inc("f6_h", 1)

                        def _gelu(sc, j=j, fc=fc, gh=gh, wv=h_v):
                            sc.wait_ge(sems["f6_h"], wv)
                            sc.activation(hTg[:, j * 1024:(j + 1) * 1024],
                                          psH[gh % 2][:], AF.Gelu,
                                          bias=b1_sb[:, fc:fc + 1]
                                          ).then_inc(sems["f6_gelu"], 1)
                        ph.add("scalar", _gelu)
                        inc("f6_gelu", 1)

                    gelu_all = cnt["f6_gelu"]

                    for tt in range(8):
                        for dj in range(4):
                            gy = cnt["f6_y"]

                            def _ymm(te, tt=tt, dj=dj, gy=gy, gw=gelu_all,
                                     w2v=w2_v, yevw=gy - 1,
                                     first=(tt == 0 and dj == 0)):
                                if first:
                                    te.wait_ge(sems["f6_gelu"], gw)
                                    te.wait_ge(sems["f6_w2"], w2v)
                                if gy >= 2:
                                    te.wait_ge(sems["f6_yev"], yevw)
                                last = None
                                for j in range(GC):
                                    last = te.matmul(
                                        psY[gy % 2][:],
                                        hTg[:, j * 1024 + tt * 128:
                                            j * 1024 + (tt + 1) * 128],
                                        w2gb[:, j * D + dj * 512:j * D + (dj + 1) * 512],
                                        start=(j == 0), stop=(j == GC - 1))
                                last.then_inc(sems["f6_y"], 1)
                            ph.add("tensor", _ymm)
                            y_v = inc("f6_y", 1)

                            def _yev(ve, g=g, tt=tt, dj=dj, gy=gy, wv=y_v,
                                     siluw=cnt["f6_silu"]):
                                ve.wait_ge(sems["f6_y"], wv)
                                ya = y_acc[:, tt * D + dj * 512:tt * D + (dj + 1) * 512]
                                if g == 0:
                                    ve.tensor_tensor(ya, psY[gy % 2][:],
                                                     b2b[:, dj * 512:(dj + 1) * 512],
                                                     ALU.add).then_inc(sems["f6_yev"], 1)
                                elif g < NG - 1:
                                    ve.tensor_tensor(ya, ya, psY[gy % 2][:], ALU.add
                                                     ).then_inc(sems["f6_yev"], 1)
                                else:
                                    if siluw >= 1 and dj == 0:
                                        ve.wait_ge(sems["f6_silu"], siluw)
                                    ve.tensor_tensor(yb[:, dj * 512:(dj + 1) * 512],
                                                     ya, psY[gy % 2][:], ALU.add
                                                     ).then_inc(sems["f6_yev"], 1)
                            ph.add("vector", _yev)
                            inc("f6_yev", 1)

                        if g == NG - 1:
                            gtt = blk * 8 + tt

                            def _sxld(sy, b=b, t0=t0, tt=tt, gtt=gtt,
                                      ow=16 * (gtt - 1)):
                                if gtt >= 2:
                                    sy.wait_ge(sems["f6_out"], ow)
                                r0 = t0 + tt * 128
                                sy.dma_start(out=xld[gtt % 2][:],
                                             in_=x_in[b, r0:r0 + 128, :]
                                             ).then_inc(sems["f6_xld"], 16)
                            ph.add("sync", _sxld)
                            xld_v = inc("f6_xld", 16)

                            def _sldf(sc, b=b, t0=t0, tt=tt, gtt=gtt,
                                      ow=16 * (gtt - 1),
                                      s4w=s4_after_b if b == 1 else s4_after_b // 2,
                                      first=(tt == 0)):
                                if first:
                                    sc.wait_ge(sems["s4_st"], s4w)
                                if gtt >= 2:
                                    sc.wait_ge(sems["f6_out"], ow)
                                r0 = t0 + tt * 128
                                sc.dma_start(out=sld[gtt % 2][:],
                                             in_=s_s[b, r0:r0 + 128, :]
                                             ).then_inc(sems["f6_sld"], 16)
                            ph.add("scalar", _sldf)
                            sld_v = inc("f6_sld", 16)

                            yev_all = cnt["f6_yev"]

                            def _silu(sc, wv=yev_all, cw=cnt["f6_cmb"]):
                                sc.wait_ge(sems["f6_yev"], wv)
                                if cw >= 1:
                                    sc.wait_ge(sems["f6_cmb"], cw)
                                sc.activation(gg[:], yb[:], AF.Silu
                                              ).then_inc(sems["f6_silu"], 1)
                            ph.add("scalar", _silu)
                            silu_v = inc("f6_silu", 1)

                            def _cmb(ve, gtt=gtt, wv=silu_v, sv=sld_v, xv=xld_v,
                                     ow=16 * (gtt - 1)):
                                ve.wait_ge(sems["f6_silu"], wv)
                                ve.wait_ge(sems["f6_sld"], sv)
                                ve.wait_ge(sems["f6_xld"], xv)
                                if gtt >= 2:
                                    ve.wait_ge(sems["f6_out"], ow)
                                ve.tensor_tensor(gg[:], gg[:], sld[gtt % 2][:],
                                                 ALU.mult)
                                ve.tensor_tensor(ost[gtt % 2][:], gg[:],
                                                 xld[gtt % 2][:], ALU.add
                                                 ).then_inc(sems["f6_cmb"], 1)
                            ph.add("vector", _cmb)
                            cmb_v = inc("f6_cmb", 1)

                            def _ost(sy, b=b, t0=t0, tt=tt, gtt=gtt, wv=cmb_v):
                                sy.wait_ge(sems["f6_cmb"], wv)
                                r0 = t0 + tt * 128
                                sy.dma_start(out=out[b, r0:r0 + 128, :],
                                             in_=ost[gtt % 2][:]
                                             ).then_inc(sems["f6_out"], 16)
                            ph.add("sync", _ost)
                            inc("f6_out", 16)

                barrier(ph)
                ph.flush(block)

            ph = Phase()
            out_total = cnt["f6_out"]

            def _fin(sy, wv=out_total):
                sy.wait_ge(sems["f6_out"], wv)
            ph.add("sync", _fin)
            ph.flush(block)

    return nc


_CACHED = {}


def kernel(L0_attn_x, L0_attn_w_qkv, L0_attn_w_gates, L0_attn_w_out,
           L0_ffn_w1, L0_ffn_b1, L0_ffn_w2, L0_ffn_b2):
    x = np.ascontiguousarray(np.asarray(L0_attn_x, dtype=np.float32))
    wqkv_a = np.ascontiguousarray(np.asarray(L0_attn_w_qkv, dtype=np.float32))
    wg_a = np.ascontiguousarray(np.asarray(L0_attn_w_gates, dtype=np.float32))
    wout_a = np.ascontiguousarray(np.asarray(L0_attn_w_out, dtype=np.float32))
    w1_a = np.ascontiguousarray(np.asarray(L0_ffn_w1, dtype=np.float32))
    b1_a = np.ascontiguousarray(np.asarray(L0_ffn_b1, dtype=np.float32))
    w2_a = np.ascontiguousarray(np.asarray(L0_ffn_w2, dtype=np.float32))
    b2_a = np.ascontiguousarray(np.asarray(L0_ffn_b2, dtype=np.float32))

    if "nc" not in _CACHED:
        _CACHED["nc"] = build()
    nc = _CACHED["nc"]

    in_maps = [{"x": x[c * B:(c + 1) * B], "wqkv": wqkv_a, "wg": wg_a,
                "wout": wout_a, "w1": w1_a, "b1": b1_a, "w2": w2_a, "b2": b2_a}
               for c in range(NCORES)]
    res = run_bass_kernel_spmd(nc, in_maps, core_ids=list(range(NCORES)))
    return np.concatenate([r["out"] for r in res.results], axis=0)


# version 09
# speedup vs baseline: 1.3499x; 1.3499x over previous
import numpy as np
import ml_dtypes
import concourse.bass as bass
import concourse.mybir as mybir
from concourse.bass_utils import run_bass_kernel_spmd
from contextlib import ExitStack
from collections import deque

F32 = mybir.dt.float32
BF16 = mybir.dt.bfloat16
AF = mybir.ActivationFunctionType
ALU = mybir.AluOpType
AX = mybir.AxisListType

NCORES = 8
B = 2
N = 2048
D = 2048
H = 16
DH = 128
DFF = 8192
DC = D // 128
FCN = DFF // 128
EPS = 1e-5
TANH_SCALE = (DH ** -0.5) / 50.0
NEG = -1.0e9

ENGS = ("sync", "tensor", "vector", "scalar", "gpsimd")


class _CloseLater:
    def __init__(self, st):
        self.st = st

    def __enter__(self):
        return self

    def __exit__(self, *a):
        self.st.close()
        return False

_PHASE_N = [0]


class Phase:
    def __init__(self):
        self.ops = {e: [] for e in ENGS}

    def add(self, eng, fn):
        self.ops[eng].append(fn)

    def flush(self, block):
        import os
        _PHASE_N[0] += 1
        limit = int(os.environ.get("K_UPTO", "1000000"))
        if _PHASE_N[0] > limit:
            self.ops = {e: [] for e in ENGS}
            return
        for e in ENGS:
            fns = self.ops[e]
            if fns:
                getattr(block, e)(lambda en, fns=fns: [f(en) for f in fns])
        self.ops = {e: [] for e in ENGS}


def build(dump=False):
    nc = bass.Bass("TRN2", target_bir_lowering=False)

    x_in = nc.dram_tensor("x", [B, N, D], F32, kind="ExternalInput").ap()
    wqkv = nc.dram_tensor("wqkv", [D, 3 * D], F32, kind="ExternalInput").ap()
    wg_in = nc.dram_tensor("wg", [D, H], F32, kind="ExternalInput").ap()
    wout_in = nc.dram_tensor("wout", [D, D], F32, kind="ExternalInput").ap()
    w1_in = nc.dram_tensor("w1", [DFF, D], F32, kind="ExternalInput").ap()
    b1_in = nc.dram_tensor("b1", [DFF], F32, kind="ExternalInput").ap()
    w2_in = nc.dram_tensor("w2", [D, DFF], F32, kind="ExternalInput").ap()
    b2_in = nc.dram_tensor("b2", [D], F32, kind="ExternalInput").ap()
    out = nc.dram_tensor("out", [B, N, D], F32, kind="ExternalOutput").ap()

    sk = "ExternalOutput" if dump else "Internal"
    xnT_s = nc.dram_tensor("xnT_s", [B, DC, 128, N], BF16, kind=sk).ap()
    qkT_s = nc.dram_tensor("qkT_s", [B, H, 2, 128, N], BF16, kind=sk).ap()
    v_s = nc.dram_tensor("v_s", [B, N, D], BF16, kind=sk).ap()
    gates_s = nc.dram_tensor("gates_s", [H, B * N], BF16, kind=sk).ap()
    s_s = nc.dram_tensor("s_s", [B, N, D], BF16, kind=sk).ap()
    w1qT_s = nc.dram_tensor("w1qT_s", [DC, 128, DFF], BF16, kind=sk).ap()
    w2qT_s = nc.dram_tensor("w2qT_s", [DFF, D], BF16, kind=sk).ap()

    identb_c = nc.inline_tensor(np.eye(128, dtype=ml_dtypes.bfloat16), "identb").ap()
    identf_c = nc.inline_tensor(np.eye(128, dtype=np.float32), "identf").ap()
    ones_row_c = nc.inline_tensor(np.ones((1, 128), dtype=ml_dtypes.bfloat16), "onesrow").ap()
    ones128_c = nc.inline_tensor(np.ones((128, 128), dtype=ml_dtypes.bfloat16), "ones128").ap()
    ones_row_f_c = nc.inline_tensor(np.ones((1, 128), dtype=np.float32), "onesrowf").ap()
    kkn = np.arange(128)[:, None]
    qqn = np.arange(512)[None, :]
    masks_np = np.stack([np.where(kkn + r * 128 > qqn, NEG, 0.0).astype(np.float32)
                         for r in range(4)])
    masks_c = nc.inline_tensor(masks_np, "masks").ap()
    taubias_c = nc.inline_tensor(np.array([[-1e-8, 0.5e-8, -0.5e-8]], dtype=np.float32),
                                 "taubias").ap()

    sems = {}
    cnt = {}

    def inc(name, k=1):
        cnt[name] += k
        return cnt[name]

    with ExitStack() as stack:
        ee = stack.enter_context

        for name in ("bar", "c_ld", "c_ld2",
                     "x_ld", "ln_st", "ln_bn", "ln_norm", "ln_tp", "ln_ev", "xnT_st",
                     "w_ld", "qk_mm", "qk_ev", "qk_st", "g_mm",
                     "v_mm", "v_ev", "v_st",
                     "at_ld", "at_sim", "at_tanh", "at_mask", "at_exp",
                     "at_av", "at_rec", "at_t1", "at_t2", "at_bc", "at_bcev",
                     "s4_mm", "s4_ev", "s4_st",
                     "t5_ld", "t5_abs", "t5_tau", "t5_tern_v", "t5_tern_g",
                     "t5_tp", "t5_evs", "t5_evv", "t5_st",
                     "f6_xnb", "f6_xnb2", "f6_xnb3", "f6_w1", "f6_w2", "f6_h", "f6_gelu",
                     "f6_y", "f6_yev", "f6_silu", "f6_sld", "f6_xld",
                     "f6_cmb", "f6_out"):
            sems[name] = ee(nc.semaphore(name))
            cnt[name] = 0

        def barrier(ph):
            tgt = cnt["bar"] + 5
            cnt["bar"] = tgt
            hh = sems["bar"]
            for e in ENGS:
                ph.add(e, lambda en, t=tgt: (en.sem_inc(hh, 1), en.wait_ge(hh, t)))

        identb = ee(nc.sbuf_tensor("identb_b", [128, 128], BF16))
        identf = ee(nc.sbuf_tensor("identf_b", [128, 128], F32))
        ones_row = ee(nc.sbuf_tensor("ones_row_b", [1, 128], BF16))
        ones128 = ee(nc.sbuf_tensor("ones128_b", [128, 128], BF16))
        ones_row_f = ee(nc.sbuf_tensor("ones_row_f_b", [1, 128], F32))
        masks = ee(nc.sbuf_tensor("masks_b", [128, 4 * 512], F32))
        b1_sb = ee(nc.sbuf_tensor("b1_sb", [128, FCN], F32))
        taub = ee(nc.sbuf_tensor("taub_b", [1, 3], F32))
        tau128 = ee(nc.sbuf_tensor("tau128_b", [128, 4], F32))

        block = ee(nc.Block())

        ph = Phase()

        def _setup(sy):
            for dst, src in ((identb[:], identb_c), (identf[:], identf_c),
                             (ones_row[:], ones_row_c), (ones128[:], ones128_c),
                             (ones_row_f[:], ones_row_f_c),
                             (taub[:], taubias_c)):
                sy.dma_start(out=dst, in_=src).then_inc(sems["c_ld"], 16)
            sy.dma_start(out=masks[:].rearrange("p (r q) -> p r q", r=4),
                         in_=masks_c.rearrange("r p q -> p r q")).then_inc(sems["c_ld"], 16)
        ph.add("sync", _setup)
        inc("c_ld", 7 * 16)

        def _setup_gp(gp):
            gp.dma_start(out=b1_sb[:], in_=b1_in.rearrange("(fc p) -> p fc", p=128),
                         allow_slow_non_contiguous=True).then_inc(sems["c_ld2"], 16)
        ph.add("gpsimd", _setup_gp)
        inc("c_ld2", 16)
        barrier(ph)
        ph.flush(block)

        passA = {"i": 0}

        def _abssrc(i):
            if i < 64:
                return w1_in[i * 128:(i + 1) * 128, :]
            j = i - 64
            dc, c = j // 4, j % 4
            return w2_in[dc * 128:(dc + 1) * 128, c * D:(c + 1) * D]

        def passA_emit(ph, ntiles, bufs):
            wta, absp = bufs["wta"], bufs["absp"]
            for _ in range(ntiles):
                i = passA["i"]
                if i >= 128:
                    return
                passA["i"] = i + 1

                def _ld(sy, i=i, tw=cnt["t5_abs"] - 3 + (i - passA["i"] + 1)):
                    if i >= 4:
                        sy.wait_ge(sems["t5_abs"], i - 3)
                    sy.dma_start(out=wta[i % 4][:], in_=_abssrc(i)
                                 ).then_inc(sems["t5_ld"], 16)
                ph.add("sync", _ld)
                ld_v = inc("t5_ld", 16)

                def _red(ve, i=i, wv=ld_v):
                    ve.wait_ge(sems["t5_ld"], wv)
                    ve.tensor_reduce(absp[:, i:i + 1], wta[i % 4][:], AX.X,
                                     ALU.add, apply_absolute_value=True
                                     ).then_inc(sems["t5_abs"], 1)
                    ve.wait_ge(sems["t5_abs"], i + 1)
                ph.add("vector", _red)
                inc("t5_abs", 1)

        def passA_final(ph, bufs):
            absp, redv = bufs["absp"], bufs["redv"]

            def _fin(ve, w0=cnt["t5_abs"]):
                ve.wait_ge(sems["t5_abs"], w0)
                ve.tensor_reduce(redv[:, 0:1], absp[:, 0:64], AX.X, ALU.add
                                 ).then_inc(sems["t5_abs"], 1)
                ve.wait_ge(sems["t5_abs"], w0 + 1)
                ve.tensor_reduce(redv[:, 1:2], absp[:, 64:128], AX.X, ALU.add
                                 ).then_inc(sems["t5_abs"], 1)
                ve.wait_ge(sems["t5_abs"], w0 + 2)
            ph.add("vector", _fin)
            inc("t5_abs", 2)

        tern_masked_exp_after = []

        for b in range(B):
            tern = (b == 0)
            with ExitStack() as sca:
                ea = sca.enter_context
                xnT_sb = ea(nc.sbuf_tensor(f"xnT_sb{b}", [128, DC * N], BF16))
                wvh = ea(nc.sbuf_tensor(f"wvh{b}", [128, DC * 1024], BF16))
                wq4 = [ea(nc.sbuf_tensor(f"wq4_{b}_{i}", [128, DC * 256], BF16))
                       for i in range(2)]
                xf = [ea(nc.sbuf_tensor(f"xf{b}_{i}", [128, D], F32)) for i in range(2)]
                import os as _os0
                xn_t = [ea(nc.sbuf_tensor(
                    f"xnt{b}_0", [128, D],
                    F32 if _os0.environ.get("K_OLDP0", "1") == "1" else BF16))]
                bnst = ea(nc.sbuf_tensor(f"bnst{b}", [128, 24], F32))
                bnmv = ea(nc.sbuf_tensor(f"bnmv{b}", [128, 2], F32))
                veps = ea(nc.sbuf_tensor(f"veps{b}", [128, 1], F32))
                sdv = ea(nc.sbuf_tensor(f"sdv{b}", [128, 1], F32))
                rinv = ea(nc.sbuf_tensor(f"rinv{b}", [128, 1], F32))
                stage = [ea(nc.sbuf_tensor(f"stageA{b}_{i}", [128, D], BF16))
                         for i in range(2)]
                gstage = ea(nc.sbuf_tensor(f"gstage{b}", [16, N], BF16))
                wgsb = ea(nc.sbuf_tensor(f"wgsb{b}", [128, DC * 16], BF16))
                if tern:
                    pa_bufs = {
                        "wta": [ea(nc.sbuf_tensor(f"wta{i}", [128, D], F32))
                                for i in range(4)],
                        "absp": ea(nc.sbuf_tensor("absp", [128, 128], F32)),
                        "redv": ea(nc.sbuf_tensor("redv", [128, 2], F32)),
                        "onesf": ea(nc.sbuf_tensor("onesf", [128, 1], F32)),
                        "tau11": ea(nc.sbuf_tensor("tau11", [1, 4], F32)),
                        "tauA": ea(nc.sbuf_tensor("tauA", [1, 2], F32)),
                    }

                xnT_view = xnT_sb[:].rearrange("p (dc t) -> p dc t", dc=DC)

                import os as _os
                _nopa = _os.environ.get("K_NOPA", "0") == "1"
                _oldp0 = _os.environ.get("K_OLDP0", "1") == "1"
                ps0 = ExitStack()
                if _oldp0:
                    psA = ps0.enter_context(nc.psum_tensor(f"psA{b}", [128, 2048], F32))
                    psB = ps0.enter_context(nc.psum_tensor(f"psB{b}", [128, 2048], F32))
                    psLn = None
                else:
                    psLn = ps0.enter_context(nc.psum_tensor(f"psLn{b}", [128, 1024], BF16))
                ph = Phase()
                if tern:
                    def _ones(ve):
                        ve.memset(pa_bufs["onesf"][:], 1.0)
                    ph.add("vector", _ones)
                for tt in range(16):
                    sl = tt % 2
                    b_norm = cnt["ln_norm"]

                    def _ld(sy, b=b, tt=tt, sl=sl, nw=b_norm - 1):
                        if tt >= 2:
                            sy.wait_ge(sems["ln_norm"], nw)
                        sy.dma_start(out=xf[sl][:],
                                     in_=x_in[b, tt * 128:(tt + 1) * 128, :]
                                     ).then_inc(sems["x_ld"], 16)
                    ph.add("sync", _ld)
                    x_ld_v = inc("x_ld", 16)

                    bnb = cnt["ln_bn"]

                    def _ln(ve, sl=sl, wv=x_ld_v, bnb=bnb):
                        ve.wait_ge(sems["x_ld"], wv)
                        k = bnb
                        for c in range(4):
                            ve.bn_stats(bnst[:, c * 6:(c + 1) * 6],
                                        xf[sl][:, c * 512:(c + 1) * 512]
                                        ).then_inc(sems["ln_bn"], 1)
                            k += 1
                            ve.wait_ge(sems["ln_bn"], k)
                        ve.bn_aggr(bnmv[:], bnst[:]).then_inc(sems["ln_bn"], 1)
                        k += 1
                        ve.wait_ge(sems["ln_bn"], k)
                        ve.tensor_scalar_add(veps[:], bnmv[:, 1:2], EPS
                                             ).then_inc(sems["ln_st"], 1)
                    ph.add("vector", _ln)
                    inc("ln_bn", 5)
                    v1 = inc("ln_st", 1)

                    def _sq(sc, wv=v1):
                        sc.wait_ge(sems["ln_st"], wv)
                        sc.sqrt(sdv[:], veps[:]).then_inc(sems["ln_st"], 1)
                    ph.add("scalar", _sq)
                    v2 = inc("ln_st", 1)

                    def _nrm(ve, sl=sl, wv=v2, tpw=cnt["ln_tp"], tt=tt):
                        ve.wait_ge(sems["ln_st"], wv)
                        ve.reciprocal(rinv[:], sdv[:])
                        if tt >= 1:
                            ve.wait_ge(sems["ln_tp"], tpw)
                        ve.tensor_scalar(xn_t[0][:], xf[sl][:], bnmv[:, 0:1],
                                         rinv[:], ALU.subtract, ALU.mult
                                         ).then_inc(sems["ln_norm"], 1)
                    ph.add("vector", _nrm)
                    norm_v = inc("ln_norm", 1)

                    for g in range(4):
                        gi0 = cnt["ln_tp"]

                        def _tp(te, g=g, wv=norm_v, gi0=gi0):
                            te.wait_ge(sems["ln_norm"], wv)
                            if _oldp0:
                                pbank = (psA if gi0 % 2 == 0 else psB)[:, 0:512]
                                idt = identf
                            else:
                                pbank = psLn[:, (gi0 % 2) * 512:(gi0 % 2 + 1) * 512]
                                idt = identb
                            if gi0 >= 2:
                                te.wait_ge(sems["ln_ev"], gi0 - 1)
                            last = None
                            for j in range(4):
                                dc = g * 4 + j
                                last = te.transpose(pbank[:, j * 128:(j + 1) * 128],
                                                    xn_t[0][:, dc * 128:(dc + 1) * 128],
                                                    idt[:])
                            last.then_inc(sems["ln_tp"], 1)
                        ph.add("tensor", _tp)
                        tp_v = inc("ln_tp", 1)

                        def _ev(sc, g=g, tt=tt, gi0=gi0, wv=tp_v):
                            sc.wait_ge(sems["ln_tp"], wv)
                            if _oldp0:
                                pbank = (psA if gi0 % 2 == 0 else psB)[:, 0:512]
                            else:
                                pbank = psLn[:, (gi0 % 2) * 512:(gi0 % 2 + 1) * 512]
                            dst = xnT_view[:, g * 4:(g + 1) * 4, tt * 128:(tt + 1) * 128]
                            src = pbank.rearrange("p (j t) -> p j t", j=4)
                            sc.copy(dst, src).then_inc(sems["ln_ev"], 1)
                        ph.add("scalar", _ev)
                        inc("ln_ev", 1)

                    if tern and not _nopa:
                        passA_emit(ph, 1, pa_bufs)

                def _xst(sy, b=b, wv=cnt["ln_ev"]):
                    sy.wait_ge(sems["ln_ev"], wv)
                    sy.dma_start(out=xnT_s[b].rearrange("dc p t -> p dc t"),
                                 in_=xnT_view).then_inc(sems["xnT_st"], 16)
                ph.add("sync", _xst)
                inc("xnT_st", 16)
                barrier(ph)
                ph.flush(block)
                if _oldp0:
                    sca.enter_context(_CloseLater(ps0))
                else:
                    ps0.close()
                    psAB = sca.enter_context(ExitStack())
                    psA = psAB.enter_context(nc.psum_tensor(f"psA{b}", [128, 2048], F32))
                    psB = psAB.enter_context(nc.psum_tensor(f"psB{b}", [128, 2048], F32))

                ph = Phase()
                b_qst = cnt["qk_st"]
                for qk in range(2):
                    for qu in range(8):
                        li = qk * 8 + qu

                        def _wld(gp, qk=qk, qu=qu, li=li,
                                 mmw=cnt["qk_mm"] - 2):
                            if li >= 2:
                                gp.wait_ge(sems["qk_mm"], mmw)
                            gp.dma_start(
                                out=wq4[li % 2][:].rearrange("p (dc m) -> p dc m", dc=DC),
                                in_=wqkv[:, qk * D + qu * 256: qk * D + (qu + 1) * 256
                                         ].rearrange("(dc p) m -> p dc m", p=128)
                            ).then_inc(sems["w_ld"], 16)
                        ph.add("gpsimd", _wld)
                        w_ld_v = inc("w_ld", 16)

                        for hs in range(2):
                            h = qu * 2 + hs
                            g_idx = qk * 16 + h

                            def _mm(te, li=li, hs=hs, wv=w_ld_v,
                                    evw=cnt["qk_ev"] - 1, g_idx=g_idx):
                                te.wait_ge(sems["w_ld"], wv)
                                pbank = psA if g_idx % 2 == 0 else psB
                                if g_idx >= 2:
                                    te.wait_ge(sems["qk_ev"], evw)
                                last = None
                                for dc in range(DC):
                                    lhs = wq4[li % 2][:].rearrange(
                                        "p (dc m) -> p dc m", dc=DC)[
                                        :, dc, hs * 128:(hs + 1) * 128]
                                    for tq in range(4):
                                        last = te.matmul(
                                            pbank[:, tq * 512:(tq + 1) * 512], lhs,
                                            xnT_view[:, dc, tq * 512:(tq + 1) * 512],
                                            start=(dc == 0), stop=(dc == DC - 1))
                                last.then_inc(sems["qk_mm"], 1)
                            ph.add("tensor", _mm)
                            mm_v = inc("qk_mm", 1)

                            def _ev(sc, g_idx=g_idx, wv=mm_v,
                                    stw=cnt["qk_st"] - 16):
                                sc.wait_ge(sems["qk_mm"], wv)
                                pbank = psA if g_idx % 2 == 0 else psB
                                if g_idx >= 2:
                                    sc.wait_ge(sems["qk_st"], stw)
                                sc.copy(stage[g_idx % 2][:], pbank[:]
                                        ).then_inc(sems["qk_ev"], 1)
                            ph.add("scalar", _ev)
                            ev_v = inc("qk_ev", 1)

                            def _st(sy, b=b, h=h, qk=qk, g_idx=g_idx, wv=ev_v):
                                sy.wait_ge(sems["qk_ev"], wv)
                                sy.dma_start(out=qkT_s[b, h, qk],
                                             in_=stage[g_idx % 2][:]
                                             ).then_inc(sems["qk_st"], 16)
                            ph.add("sync", _st)
                            inc("qk_st", 16)

                        if tern:
                            passA_emit(ph, 5, pa_bufs)

                def _wgld(gp):
                    gp.dma_start(out=wgsb[:].rearrange("p (dc g) -> p dc g", g=16),
                                 in_=wg_in.rearrange("(dc p) g -> p dc g", p=128)
                                 ).then_inc(sems["w_ld"], 16)
                ph.add("gpsimd", _wgld)
                wg_v = inc("w_ld", 16)

                def _gmm(te, wv=wg_v, evw=cnt["qk_ev"]):
                    te.wait_ge(sems["w_ld"], wv)
                    te.wait_ge(sems["qk_ev"], evw)
                    last = None
                    for tq in range(4):
                        for dc in range(DC):
                            last = te.matmul(
                                psA[0:16, tq * 512:(tq + 1) * 512],
                                wgsb[:].rearrange("p (dc g) -> p dc g", g=16)[:, dc, :],
                                xnT_view[:, dc, tq * 512:(tq + 1) * 512],
                                start=(dc == 0), stop=(dc == DC - 1))
                    last.then_inc(sems["g_mm"], 1)
                ph.add("tensor", _gmm)
                g1 = inc("g_mm", 1)

                def _gsig(sc, wv=g1):
                    sc.wait_ge(sems["g_mm"], wv)
                    sc.activation(gstage[:], psA[0:16, :], AF.Sigmoid
                                  ).then_inc(sems["g_mm"], 1)
                ph.add("scalar", _gsig)
                g2 = inc("g_mm", 1)

                def _gst(sy, b=b, wv=g2):
                    sy.wait_ge(sems["g_mm"], wv)
                    sy.dma_start(out=gates_s[:, b * N:(b + 1) * N],
                                 in_=gstage[:]).then_inc(sems["qk_st"], 16)
                ph.add("sync", _gst)
                inc("qk_st", 16)
                barrier(ph)
                ph.flush(block)

                ph = Phase()
                wvh_view = wvh[:].rearrange("p (dc m) -> p dc m", dc=DC)
                for half in range(2):
                    def _wvld(gp, half=half, mmw=cnt["v_mm"]):
                        if half == 1:
                            gp.wait_ge(sems["v_mm"], mmw)
                        gp.dma_start(
                            out=wvh_view,
                            in_=wqkv[:, 2 * D + half * 1024:2 * D + (half + 1) * 1024
                                     ].rearrange("(dc p) m -> p dc m", p=128)
                        ).then_inc(sems["w_ld"], 16)
                    ph.add("gpsimd", _wvld)
                    wv_v = inc("w_ld", 16)

                    for tt in range(16):
                        i2 = half * 16 + tt

                        def _vmm(te, tt=tt, i2=i2, wv=wv_v, evw=cnt["v_ev"] - 1):
                            te.wait_ge(sems["w_ld"], wv)
                            pbank = psA if i2 % 2 == 0 else psB
                            if i2 >= 2:
                                te.wait_ge(sems["v_ev"], evw)
                            last = None
                            for mj in range(2):
                                for dc in range(DC):
                                    last = te.matmul(
                                        pbank[:, mj * 512:(mj + 1) * 512],
                                        xnT_view[:, dc, tt * 128:(tt + 1) * 128],
                                        wvh_view[:, dc, mj * 512:(mj + 1) * 512],
                                        start=(dc == 0), stop=(dc == DC - 1))
                            last.then_inc(sems["v_mm"], 1)
                        ph.add("tensor", _vmm)
                        vm_v = inc("v_mm", 1)

                        def _vev(sc, i2=i2, wv=vm_v, stw=cnt["v_st"] - 16):
                            sc.wait_ge(sems["v_mm"], wv)
                            pbank = psA if i2 % 2 == 0 else psB
                            if i2 >= 2:
                                sc.wait_ge(sems["v_st"], stw)
                            sc.copy(stage[i2 % 2][:, 0:1024], pbank[:, 0:1024]
                                    ).then_inc(sems["v_ev"], 1)
                        ph.add("scalar", _vev)
                        ve_v = inc("v_ev", 1)

                        def _vst(sy, b=b, tt=tt, half=half, i2=i2, wv=ve_v):
                            sy.wait_ge(sems["v_ev"], wv)
                            sy.dma_start(out=v_s[b, tt * 128:(tt + 1) * 128,
                                                 half * 1024:(half + 1) * 1024],
                                         in_=stage[i2 % 2][:, 0:1024]
                                         ).then_inc(sems["v_st"], 16)
                        ph.add("sync", _vst)
                        inc("v_st", 16)

                        if tern:
                            passA_emit(ph, 1, pa_bufs)

                if tern:
                    passA_final(ph, pa_bufs)
                    redv, onesf = pa_bufs["redv"], pa_bufs["onesf"]
                    tau11, tauA = pa_bufs["tau11"], pa_bufs["tauA"]

                    def _taumm(te, wv=cnt["t5_abs"], evw=cnt["v_ev"]):
                        te.wait_ge(sems["t5_abs"], wv)
                        te.wait_ge(sems["v_ev"], evw)
                        te.matmul(psB[0:1, 0:1], redv[:, 0:1], onesf[:],
                                  start=True, stop=True)
                        te.matmul(psB[0:1, 1:2], redv[:, 1:2], onesf[:], start=True,
                                  stop=True).then_inc(sems["t5_tau"], 1)
                    ph.add("tensor", _taumm)
                    t1v = inc("t5_tau", 1)

                    def _tauv(sc, wv=t1v):
                        sc.wait_ge(sems["t5_tau"], wv)
                        sc.activation(tauA[0:1, :], psB[0:1, 0:2], AF.Relu,
                                      scale=1.0 / (DFF * D), bias=taub[0:1, 0:1])
                        sc.activation(tau11[0:1, 0:2], tauA[0:1, :], AF.Identity,
                                      scale=0.5, bias=taub[0:1, 1:2])
                        sc.activation(tau11[0:1, 2:4], tauA[0:1, :], AF.Identity,
                                      scale=-0.5, bias=taub[0:1, 2:3]
                                      ).then_inc(sems["t5_tau"], 1)
                    ph.add("scalar", _tauv)
                    t2v = inc("t5_tau", 1)

                    def _taubc(te, wv=t2v):
                        te.wait_ge(sems["t5_tau"], wv)
                        te.matmul(psB[:, 0:4], ones_row_f[:], tau11[0:1, 0:4],
                                  start=True, stop=True).then_inc(sems["t5_tau"], 1)
                    ph.add("tensor", _taubc)
                    t3v = inc("t5_tau", 1)

                    def _tauev(sc, wv=t3v):
                        sc.wait_ge(sems["t5_tau"], wv)
                        sc.copy(tau128[:], psB[:, 0:4]).then_inc(sems["t5_tau"], 1)
                    ph.add("scalar", _tauev)
                    inc("t5_tau", 1)
                barrier(ph)
                ph.flush(block)

            import os as _osB
            if _osB.environ.get("K_NOTB", "0") == "1":
                tern = False
            with ExitStack() as scb:
                eb = scb.enter_context
                oaT = eb(nc.sbuf_tensor(f"oaT{b}", [128, H * N], BF16))
                qT = [eb(nc.sbuf_tensor(f"qT{b}_{i}", [128, N], BF16)) for i in range(2)]
                kTb = [eb(nc.sbuf_tensor(f"kT{b}_{i}", [128, N], BF16)) for i in range(2)]
                vvb = [eb(nc.sbuf_tensor(f"vv{b}_{i}", [128, N], BF16)) for i in range(2)]
                grow = [eb(nc.sbuf_tensor(f"grow{b}_{i}", [1, N], BF16)) for i in range(2)]
                ttg = [eb(nc.sbuf_tensor(f"ttg{b}_{i}", [128, 1024], F32)) for i in range(2)]
                tmg = [eb(nc.sbuf_tensor(f"tmg{b}_{i}", [128, 1024], F32)) for i in range(2)]
                pTg = [eb(nc.sbuf_tensor(f"pTg{b}_{i}", [128, 1024], BF16)) for i in range(2)]
                bcg = [eb(nc.sbuf_tensor(f"bcg{b}_{i}", [128, 512], BF16)) for i in range(2)]
                rrowB = eb(nc.sbuf_tensor(f"rrowB{b}", [128, 512], F32))
                tmpO = eb(nc.sbuf_tensor(f"tmpO{b}", [128, 512], BF16))

                ps3 = ExitStack()
                if tern:
                    e3t = ps3.enter_context
                    tb = {
                        "wtile": [e3t(nc.sbuf_tensor(f"twt{i}", [128, 1024], F32))
                                  for i in range(2)],
                        "terna_g": e3t(nc.sbuf_tensor("terna_g", [128, 1024], F32)),
                        "ternb_g": e3t(nc.sbuf_tensor("ternb_g", [128, 1024], F32)),
                        "ternt": [e3t(nc.sbuf_tensor(f"ternt{i}", [128, 1024], BF16))
                                  for i in range(16)],
                        "tstage": [e3t(nc.sbuf_tensor(f"tstage{i}", [128, 1024], BF16))
                                   for i in range(2)],
                    }
                psSt = [ps3.enter_context(nc.psum_tensor(f"psS{b}_{i}", [128, 1024], F32))
                        for i in range(2)]
                psS = [psSt[0][:], psSt[1][:]]
                psO = ps3.enter_context(nc.psum_tensor(f"psO{b}", [128, 512], F32))[:]
                psRB = ps3.enter_context(nc.psum_tensor(f"psRB{b}", [128, 512], F32))[:]
                psBc = ps3.enter_context(nc.psum_tensor(f"psBc{b}", [128, 512], F32))[:]
                psTt = ps3.enter_context(nc.psum_tensor(f"psTt{b}", [128, 512], BF16))
                psT2 = [psTt[:], psTt[:]]

                cnt_t5ld_base = [cnt["t5_ld"]]
                import os as _osT
                _tblv = {"ld": 1, "cmp": 2, "tp": 3, "ev": 4, "st": 5,
                         "full": 9}[_osT.environ.get("K_TB", "full")]

                def _tern_done(en, i, off=1):
                    en.wait_ge(sems["t5_tern_g"], i + off)

                def tern_load(sy, i):
                    mi, fc, half = i // 128, (i % 128) // 2, i % 2
                    if i >= 2 and _tblv >= 2:
                        _tern_done(sy, i - 2)
                    if mi == 0:
                        sy.dma_start(out=tb["wtile"][i % 2][:],
                                     in_=w1_in[fc * 128:(fc + 1) * 128,
                                               half * 1024:(half + 1) * 1024]
                                     ).then_inc(sems["t5_ld"], 16)
                    else:
                        sy.dma_start(
                            out=tb["wtile"][i % 2][:].rearrange(
                                "p (dc f) -> p dc f", dc=8),
                            in_=w2_in[half * 1024:(half + 1) * 1024,
                                      fc * 128:(fc + 1) * 128
                                      ].rearrange("(dc p) f -> p dc f", p=128)
                        ).then_inc(sems["t5_ld"], 16)

                def tern_cmp(en, i, first, ta):
                    if _tblv < 2:
                        return
                    mi = i // 128
                    en.wait_ge(sems["t5_ld"], 16 * (i + 1) + cnt_t5ld_base[0])
                    if first:
                        en.wait_ge(sems["t5_tau"], 4)
                    if i >= 16 and _tblv >= 3:
                        en.wait_ge(sems["t5_tp"], 2 * (i - 16) + 2)
                    en.tensor_scalar(ta[0][:], tb["wtile"][i % 2][:],
                                     tau128[:, mi:mi + 1], 1.0,
                                     ALU.is_gt, ALU.subtract)
                    en.tensor_scalar(ta[1][:], tb["wtile"][i % 2][:],
                                     tau128[:, 2 + mi:3 + mi], None, ALU.is_ge)
                    en.tensor_tensor(tb["ternt"][i % 16][:], ta[0][:], ta[1][:],
                                     ALU.add).then_inc(sems["t5_tern_g"], 1)

                def tern_tp(te, i, g2, G):
                    if _tblv < 3:
                        return
                    _tern_done(te, i)
                    if _tblv >= 4:
                        if g2 == 0:
                            if i >= 1:
                                te.wait_ge(sems["t5_evv"], i)
                        else:
                            te.wait_ge(sems["t5_evs"], i + 1)
                    bank = psT2[G % 2]
                    last = None
                    for j in range(4):
                        dcs = g2 * 4 + j
                        last = te.transpose(bank[:, j * 128:(j + 1) * 128],
                                            tb["ternt"][i % 16][:, dcs * 128:(dcs + 1) * 128],
                                            identb[:])
                    last.then_inc(sems["t5_tp"], 1)

                def tern_evac(en, i, g2, G):
                    if _tblv < 4:
                        return
                    en.wait_ge(sems["t5_tp"], G + 1)
                    if g2 == 0 and i >= 2 and _tblv >= 5:
                        en.wait_ge(sems["t5_st"], 16 * (i - 1))
                    dst = tb["tstage"][i % 2][:].rearrange(
                        "p (g2 j f) -> p g2 j f", g2=2, j=4)[:, g2]
                    src = psT2[G % 2].rearrange("p (j f) -> p j f", j=4)
                    sem = sems["t5_evs"] if g2 == 0 else sems["t5_evv"]
                    en.copy(dst, src).then_inc(sem, 1)

                def tern_store(sy, i):
                    if _tblv < 5:
                        return
                    mi, fc, half = i // 128, (i % 128) // 2, i % 2
                    sy.wait_ge(sems["t5_evs"], i + 1)
                    sy.wait_ge(sems["t5_evv"], i + 1)
                    if mi == 0:
                        dst = w1qT_s[half * 8:(half + 1) * 8, :,
                                     fc * 128:(fc + 1) * 128
                                     ].rearrange("dc p f -> p dc f")
                        sy.dma_start(out=dst, in_=tb["tstage"][i % 2][:].rearrange(
                            "p (dc f) -> p dc f", dc=8)).then_inc(sems["t5_st"], 16)
                    else:
                        sy.dma_start(out=w2qT_s[fc * 128:(fc + 1) * 128,
                                                half * 1024:(half + 1) * 1024],
                                     in_=tb["tstage"][i % 2][:]
                                     ).then_inc(sems["t5_st"], 16)

                ph = Phase()
                v_st_total = cnt["v_st"]
                qk_st_total = cnt["qk_st"]
                for h in range(H):
                    sl = h % 2
                    Hg = b * H + h

                    def _atld(sc, b=b, h=h, sl=sl, Hg=Hg, otw=cnt["at_t2"] - 8,
                              first=(h == 0), vw=v_st_total, qw=qk_st_total):
                        if first:
                            sc.wait_ge(sems["v_st"], vw)
                            sc.wait_ge(sems["qk_st"], qw)
                        if Hg >= 2:
                            sc.wait_ge(sems["at_t2"], otw)
                        sc.dma_start(out=qT[sl][:], in_=qkT_s[b, h, 0]
                                     ).then_inc(sems["at_ld"], 16)
                        sc.dma_start(out=kTb[sl][:], in_=qkT_s[b, h, 1]
                                     ).then_inc(sems["at_ld"], 16)
                        sc.dma_start(out=vvb[sl][:].rearrange("p (kt j) -> p kt j", kt=16),
                                     in_=v_s[b, :, h * 128:(h + 1) * 128
                                             ].rearrange("(kt p) j -> p kt j", p=128)
                                     ).then_inc(sems["at_ld"], 16)
                        sc.dma_start(out=grow[sl][:],
                                     in_=gates_s[h:h + 1, b * N:(b + 1) * N]
                                     ).then_inc(sems["at_ld"], 16)
                    ph.add("scalar", _atld)
                    at_ld_v = inc("at_ld", 64)

                    if tern:
                        c_lo = h * 16
                        pe_sprinkle = deque()
                        if h >= 1:
                            p_lo = (h - 1) * 16
                            for ii in range(p_lo, p_lo + 16):
                                for g2 in range(2):
                                    G = 2 * ii + g2
                                    pe_sprinkle.append(
                                        lambda te, ii=ii, g2=g2, G=G:
                                        tern_tp(te, ii, g2, G))
                            inc("t5_tp", 32)
                            evac_sched = [(ii, g2, 2 * ii + g2)
                                          for ii in range(p_lo, p_lo + 16)
                                          for g2 in range(2)]
                            inc("t5_evs", 16)
                            inc("t5_evv", 16)
                            store_sched = list(range(p_lo, p_lo + 16))
                            inc("t5_st", 16 * 16)
                        else:
                            evac_sched = []
                            store_sched = []
                        load_sched = list(range(c_lo, c_lo + 16))
                        inc("t5_ld", 16 * 16)
                        cmp_all = list(load_sched)
                        inc("t5_tern_g", 16)

                        def _tload(sy, load_sched=load_sched,
                                   store_sched=store_sched):
                            li = list(load_sched)
                            si = list(store_sched)
                            while li or si:
                                for _ in range(2):
                                    if li:
                                        tern_load(sy, li.pop(0))
                                for _ in range(2):
                                    if si:
                                        tern_store(sy, si.pop(0))
                        ph.add("sync", _tload)

                        cmp_q = [cmp_all[4 * q:4 * q + 4] for q in range(4)]
                        evac_q = [evac_sched[8 * q:8 * q + 8] for q in range(4)]
                    else:
                        pe_sprinkle = deque()
                        cmp_q = [[] for _ in range(4)]
                        evac_q = [[] for _ in range(4)]

                    for qb in range(4):
                        nkt = 4 * (qb + 1)
                        ngrp = nkt // 2
                        Q = cnt["at_bc"]
                        sim0 = cnt["at_sim"]
                        exp0 = cnt["at_exp"]
                        av0 = cnt["at_av"]
                        t1_prev = cnt["at_t1"]
                        rec_prev = cnt["at_rec"]
                        bcev_prev = cnt["at_bcev"]

                        def _pe(te, sl=sl, qb=qb, nkt=nkt, ngrp=ngrp,
                                ldv=at_ld_v, sim0=sim0, exp0=exp0,
                                Q=Q, t1_prev=t1_prev, rec_prev=rec_prev,
                                bcev_prev=bcev_prev,
                                sprinkle=(list(pe_sprinkle)[8 * qb:8 * qb + 8]
                                          if tern else [])):
                            sp = list(sprinkle)

                            def pop_sp():
                                if sp:
                                    sp.pop(0)(te)

                            te.wait_ge(sems["at_ld"], ldv)
                            if Q >= 1:
                                te.wait_ge(sems["at_bcev"], bcev_prev)
                            te.matmul(psBc, ones_row[:],
                                      grow[sl][0:1, qb * 512:(qb + 1) * 512],
                                      start=True, stop=True).then_inc(sems["at_bc"], 1)

                            def av_rs(g):
                                j0, j1 = 2 * g, 2 * g + 1
                                pt = pTg[(exp0 + g) % 2]
                                te.wait_ge(sems["at_exp"], exp0 + g + 1)
                                if g == 0 and Q >= 1:
                                    te.wait_ge(sems["at_t1"], t1_prev)
                                    te.wait_ge(sems["at_rec"], rec_prev)
                                te.matmul(psO, vvb[sl][:, j0 * 128:(j0 + 1) * 128],
                                          pt[:, 0:512], start=(j0 == 0), stop=False)
                                te.matmul(psO, vvb[sl][:, j1 * 128:(j1 + 1) * 128],
                                          pt[:, 512:1024], start=False,
                                          stop=(j1 == nkt - 1))
                                pop_sp()
                                te.matmul(psRB, ones128[:], pt[:, 0:512],
                                          start=(j0 == 0), stop=False)
                                te.matmul(psRB, ones128[:], pt[:, 512:1024],
                                          start=False, stop=(j1 == nkt - 1)
                                          ).then_inc(sems["at_av"], 1)
                                pop_sp()

                            for g in range(ngrp):
                                gsi = sim0 + g
                                if gsi >= 2:
                                    te.wait_ge(sems["at_tanh"], gsi - 1)
                                te.matmul(psS[gsi % 2][:, 0:512],
                                          kTb[sl][:, (2 * g) * 128:(2 * g + 1) * 128],
                                          qT[sl][:, qb * 512:(qb + 1) * 512],
                                          start=True, stop=True)
                                te.matmul(psS[gsi % 2][:, 512:1024],
                                          kTb[sl][:, (2 * g + 1) * 128:(2 * g + 2) * 128],
                                          qT[sl][:, qb * 512:(qb + 1) * 512],
                                          start=True, stop=True
                                          ).then_inc(sems["at_sim"], 1)
                                if g >= 1:
                                    av_rs(g - 1)
                            av_rs(ngrp - 1)
                            while sp:
                                sp.pop(0)(te)
                        ph.add("tensor", _pe)
                        inc("at_bc", 1)
                        for g in range(ngrp):
                            inc("at_sim", 1)
                            inc("at_av", 1)

                        sc_evs = evac_q[qb]

                        for g in range(ngrp):
                            rel0 = 2 * g - 4 * qb
                            masked = rel0 >= 0
                            gti = sim0 + g
                            gei = exp0 + g

                            def _tanh(sc, g=g, sv=sim0 + g + 1, gti=gti,
                                      expw=gti - 1):
                                sc.wait_ge(sems["at_sim"], sv)
                                if gti >= 2:
                                    sc.wait_ge(sems["at_exp"], expw)
                                sc.activation(ttg[gti % 2][:], psS[gti % 2],
                                              AF.Tanh, scale=TANH_SCALE
                                              ).then_inc(sems["at_tanh"], 1)
                            ph.add("scalar", _tanh)
                            tanh_v = inc("at_tanh", 1)

                            if masked:
                                mi = cnt["at_mask"]
                                slot_w = (tern_masked_exp_after[mi - 2]
                                          if mi >= 2 else 0)
                                tern_masked_exp_after.append(gei + 1)

                                def _mask(ve, rel0=rel0, wv=tanh_v, mi=mi,
                                          gti=gti, slot_w=slot_w):
                                    ve.wait_ge(sems["at_tanh"], wv)
                                    if slot_w >= 1:
                                        ve.wait_ge(sems["at_exp"], slot_w)
                                    t1 = None
                                    for s in range(2):
                                        rel = rel0 + s
                                        w = (rel + 1) * 128
                                        t1 = ve.tensor_tensor(
                                            tmg[mi % 2][:, s * 512:s * 512 + w],
                                            ttg[gti % 2][:, s * 512:s * 512 + w],
                                            masks[:, rel * 512:rel * 512 + w],
                                            ALU.add)
                                        if w < 512:
                                            t1 = ve.tensor_copy(
                                                tmg[mi % 2][:, s * 512 + w:(s + 1) * 512],
                                                ttg[gti % 2][:, s * 512 + w:(s + 1) * 512])
                                    t1.then_inc(sems["at_mask"], 1)
                                ph.add("vector", _mask)
                                mask_v = inc("at_mask", 1)
                            else:
                                mask_v = None

                            def _exp(sc, masked=masked, mask_v=mask_v,
                                     gti=gti, gei=gei,
                                     mi=(cnt["at_mask"] - 1 if masked else 0),
                                     evp=sc_evs[2 * g:2 * g + 2]):
                                if masked:
                                    sc.wait_ge(sems["at_mask"], mask_v)
                                if gei >= 2:
                                    sc.wait_ge(sems["at_av"], gei - 1)
                                src = tmg[mi % 2][:] if masked else ttg[gti % 2][:]
                                sc.activation(pTg[gei % 2][:], src, AF.Exp,
                                              scale=50.0).then_inc(sems["at_exp"], 1)
                                for ev in evp:
                                    tern_evac(sc, *ev)
                            ph.add("scalar", _exp)
                            inc("at_exp", 1)

                        if len(sc_evs) > 2 * ngrp:
                            rest = sc_evs[2 * ngrp:]

                            def _scev(sc, rest=rest):
                                for ev in rest:
                                    tern_evac(sc, *ev)
                            ph.add("scalar", _scev)

                        def _bcev(ve, Q=Q, bcw=cnt["at_bc"], t2w=cnt["at_t2"] - 1):
                            ve.wait_ge(sems["at_bc"], bcw)
                            if Q >= 2:
                                ve.wait_ge(sems["at_t2"], t2w)
                            ve.tensor_copy(bcg[Q % 2][:], psBc
                                           ).then_inc(sems["at_bcev"], 1)
                        ph.add("vector", _bcev)
                        inc("at_bcev", 1)

                        av_full = cnt["at_av"]

                        def _tail(ve, h=h, qb=qb, Q=Q, wv=av_full,
                                  cmps=cmp_q[qb]):
                            ve.wait_ge(sems["at_av"], wv)
                            ve.reciprocal(rrowB[:], psRB).then_inc(sems["at_rec"], 1)
                            ve.tensor_tensor(tmpO[:], psO, rrowB[:], ALU.mult
                                             ).then_inc(sems["at_t1"], 1)
                            ve.tensor_tensor(
                                oaT[:, h * N + qb * 512:h * N + (qb + 1) * 512],
                                tmpO[:], bcg[Q % 2][:], ALU.mult
                                ).then_inc(sems["at_t2"], 1)
                            for i in cmps:
                                tern_cmp(ve, i, (h == 0 and qb == 0 and i == cmps[0]),
                                         (tb["terna_g"], tb["ternb_g"]))
                        ph.add("vector", _tail)
                        inc("at_rec", 1)
                        inc("at_t1", 1)
                        inc("at_t2", 1)

                if tern:
                    p_lo = 15 * 16

                    def _ttail_pe(te, p_lo=p_lo):
                        for ii in range(p_lo, p_lo + 16):
                            for g2 in range(2):
                                G = 2 * ii + g2
                                tern_tp(te, ii, g2, G)
                    ph.add("tensor", _ttail_pe)
                    inc("t5_tp", 32)

                    def _ttail_evs(sc, p_lo=p_lo):
                        for ii in range(p_lo, p_lo + 16):
                            tern_evac(sc, ii, 0, 2 * ii)
                            tern_evac(sc, ii, 1, 2 * ii + 1)
                    ph.add("scalar", _ttail_evs)
                    inc("t5_evs", 16)
                    inc("t5_evv", 16)

                    def _ttail_st(sy, p_lo=p_lo):
                        for ii in range(p_lo, p_lo + 16):
                            tern_store(sy, ii)
                    ph.add("sync", _ttail_st)
                    inc("t5_st", 16 * 16)

                barrier(ph)
                ph.flush(block)
                ps3.close()

                with ExitStack() as scc:
                    ec = scc.enter_context
                    woutd = [ec(nc.sbuf_tensor(f"woutd{b}_{i}", [128, H * 512], BF16))
                             for i in range(2)]
                    stageB = [ec(nc.sbuf_tensor(f"stageB{b}_{i}", [128, 512], BF16))
                              for i in range(2)]
                    psP4 = ec(nc.psum_tensor(f"psP4_{b}", [128, 2048], F32))
                    ph = Phase()
                    b_smm = cnt["s4_mm"]
                    b_sev = cnt["s4_ev"]
                    b_sst = cnt["s4_st"]
                    for dj in range(4):
                        def _wodld(gp, b=b, dj=dj, mmw=b_smm + 16 * (dj - 1)):
                            if dj >= 2:
                                gp.wait_ge(sems["s4_mm"], mmw)
                            gp.dma_start(
                                out=woutd[dj % 2][:].rearrange("p (hc m) -> p hc m", hc=H),
                                in_=wout_in[:, dj * 512:(dj + 1) * 512
                                            ].rearrange("(hc p) m -> p hc m", p=128)
                            ).then_inc(sems["w_ld"], 16)
                        ph.add("gpsimd", _wodld)
                        wod_v = inc("w_ld", 16)

                        for tt in range(16):
                            i4 = dj * 16 + tt

                            def _smm(te, dj=dj, tt=tt, i4=i4, wv=wod_v,
                                     evw=b_sev + i4 - 3):
                                te.wait_ge(sems["w_ld"], wv)
                                if i4 >= 4:
                                    te.wait_ge(sems["s4_ev"], evw)
                                pb = psP4[:, (i4 % 4) * 512:(i4 % 4 + 1) * 512]
                                last = None
                                for hc in range(H):
                                    last = te.matmul(
                                        pb,
                                        oaT[:, hc * N + tt * 128:hc * N + (tt + 1) * 128],
                                        woutd[dj % 2][:].rearrange(
                                            "p (hc m) -> p hc m", hc=H)[:, hc, :],
                                        start=(hc == 0), stop=(hc == H - 1))
                                last.then_inc(sems["s4_mm"], 1)
                            ph.add("tensor", _smm)
                            sm_v = inc("s4_mm", 1)

                            def _sev(sc, i4=i4, wv=sm_v, stw=b_sst + 16 * (i4 - 1)):
                                sc.wait_ge(sems["s4_mm"], wv)
                                if i4 >= 2:
                                    sc.wait_ge(sems["s4_st"], stw)
                                pb = psP4[:, (i4 % 4) * 512:(i4 % 4 + 1) * 512]
                                sc.copy(stageB[i4 % 2][:], pb).then_inc(sems["s4_ev"], 1)
                            ph.add("scalar", _sev)
                            se_v = inc("s4_ev", 1)

                            def _sst(sy, b=b, dj=dj, tt=tt, i4=i4, wv=se_v):
                                sy.wait_ge(sems["s4_ev"], wv)
                                sy.dma_start(out=s_s[b, tt * 128:(tt + 1) * 128,
                                                     dj * 512:(dj + 1) * 512],
                                             in_=stageB[i4 % 2][:]).then_inc(sems["s4_st"], 16)
                            ph.add("sync", _sst)
                            inc("s4_st", 16)
                    barrier(ph)
                    ph.flush(block)

        s4_after_b = cnt["s4_st"]

        with ExitStack() as sc3:
            e3 = sc3.enter_context
            xnb = [e3(nc.sbuf_tensor(f"xnb{i}", [128, DC * 512], BF16))
                   for i in range(3)]
            w1sl = [e3(nc.sbuf_tensor(f"w1sl{i}", [128, DC * 256], BF16))
                    for i in range(2)]
            w2gb = e3(nc.sbuf_tensor("w2gb", [128, 8 * D], BF16))
            hTg = e3(nc.sbuf_tensor("hTg", [128, 8 * 1024], BF16))
            y_acc = e3(nc.sbuf_tensor("y_acc", [128, 8 * D], BF16))
            b2b = e3(nc.sbuf_tensor("b2b", [128, D], BF16))
            gg = e3(nc.sbuf_tensor("gg", [128, D], BF16))
            sld = [e3(nc.sbuf_tensor(f"sld{i}", [128, D], BF16)) for i in range(2)]
            xld = [e3(nc.sbuf_tensor(f"xld{i}", [128, D], F32)) for i in range(2)]
            psH = [e3(nc.psum_tensor(f"psH{i}", [128, 1024], F32)) for i in range(2)]
            psY = [e3(nc.psum_tensor(f"psY{i}", [128, 512], F32)) for i in range(2)]
            b2sub = ExitStack()
            b2row = b2sub.enter_context(nc.sbuf_tensor("b2row_p6", [1, D], F32))

            xnb_views = [x[:].rearrange("p (dc t) -> p dc t", dc=DC)
                         for x in xnb]
            xnb_sems = ["f6_xnb", "f6_xnb2", "f6_xnb3"]

            ph = Phase()

            def _b2ld(sy):
                sy.dma_start(out=b2row[:],
                             in_=b2_in.rearrange("(one d) -> one d", one=1)
                             ).then_inc(sems["c_ld"], 16)
            ph.add("sync", _b2ld)
            b2ld_v = inc("c_ld", 16)

            def _b2mm(te, ldv=b2ld_v):
                te.wait_ge(sems["c_ld"], ldv)
                for dj in range(4):
                    if dj >= 2:
                        te.wait_ge(sems["f6_yev"], dj - 1)
                    te.matmul(psY[dj % 2][:], ones_row_f[:],
                              b2row[0:1, dj * 512:(dj + 1) * 512],
                              start=True, stop=True).then_inc(sems["f6_y"], 1)
            ph.add("tensor", _b2mm)

            def _b2ev(sc):
                for dj in range(4):
                    sc.wait_ge(sems["f6_y"], dj + 1)
                    sc.copy(b2b[:, dj * 512:(dj + 1) * 512], psY[dj % 2][:]
                            ).then_inc(sems["f6_yev"], 1)
            ph.add("scalar", _b2ev)
            for dj in range(4):
                inc("f6_y", 1)
                inc("f6_yev", 1)
            barrier(ph)
            ph.flush(block)
            b2sub.close()
            ost = [e3(nc.sbuf_tensor(f"ost{i}", [128, D], F32)) for i in range(2)]

            NG = 8
            GC = 8
            gsl_ctr = [0]
            t5_st_w1 = 128 * 16
            t5_st_all = 256 * 16
            for blk in range(4):
                b = blk // 2
                t0 = (blk % 2) * 1024
                ph = Phase()
                h_prev_blk = cnt["f6_h"]

                xnb_binds = []
                for hf in range(2):
                    u = 2 * blk + hf
                    slot = u % 3
                    semn = xnb_sems[slot]

                    def _xnbld(sy, b=b, t0=t0, hf=hf, u=u, slot=slot, semn=semn,
                               hw=64 * ((u - 3) // 2 + 1),
                               first=(blk == 0 and hf == 0)):
                        if first:
                            sy.wait_ge(sems["xnT_st"], 16)
                        if b == 1 and t0 == 0 and hf == 0:
                            sy.wait_ge(sems["xnT_st"], 32)
                        if u >= 3:
                            sy.wait_ge(sems["f6_h"], hw)
                        sy.dma_start(out=xnb_views[slot],
                                     in_=xnT_s[b][:, :, t0 + hf * 512:t0 + hf * 512 + 512
                                                  ].rearrange("dc p t -> p dc t")
                                     ).then_inc(sems[semn], 16)
                    ph.add("sync", _xnbld)
                    xnb_binds.append((slot, semn, inc(semn, 16)))

                for g in range(NG):
                    y_prev = cnt["f6_y"]

                    for ss in range(4):
                        gsl = gsl_ctr[0]
                        gsl_ctr[0] += 1
                        sg = g * 4 + ss

                        def _w1ld(sy, sg=sg, gsl=gsl, hw=2 * (gsl - 1)):
                            if gsl == 0:
                                sy.wait_ge(sems["t5_st"], t5_st_w1)
                            if gsl >= 2:
                                sy.wait_ge(sems["f6_h"], hw)
                            sy.dma_start(
                                out=w1sl[gsl % 2][:].rearrange("p (dc f) -> p dc f", dc=DC),
                                in_=w1qT_s[:, :, sg * 256:(sg + 1) * 256
                                           ].rearrange("dc p f -> p dc f")
                            ).then_inc(sems["f6_w1"], 16)
                        ph.add("sync", _w1ld)
                        inc("f6_w1", 16)
                    w1_vals = [cnt["f6_w1"] - 16 * (3 - s) for s in range(4)]

                    def _w2ld(sc, g=g, yw=y_prev, first=(blk == 0 and g == 0)):
                        if first:
                            sc.wait_ge(sems["t5_st"], t5_st_all)
                        if yw >= 1:
                            sc.wait_ge(sems["f6_y"], yw)
                        for j in range(GC):
                            fc = g * GC + j
                            sc.dma_start(out=w2gb[:, j * D:(j + 1) * D],
                                         in_=w2qT_s[fc * 128:(fc + 1) * 128, :]
                                         ).then_inc(sems["f6_w2"], 16)
                    ph.add("scalar", _w2ld)
                    inc("f6_w2", 16 * GC)
                    w2_v = cnt["f6_w2"]

                    gelu0 = cnt["f6_gelu"]
                    for j in range(GC):
                        fc = g * GC + j
                        gh = cnt["f6_h"]

                        def _hmm(te, j=j, gh=gh, xb=list(xnb_binds), yw=y_prev,
                                 wv=w1_vals[j // 2], gsl_base=gsl_ctr[0] - 4,
                                 geluw=gh - 1):
                            te.wait_ge(sems["f6_w1"], wv)
                            if j == 0:
                                for slot, semn, xv in xb:
                                    te.wait_ge(sems[semn], xv)
                                if yw >= 1:
                                    te.wait_ge(sems["f6_y"], yw)
                            if gh >= 2:
                                te.wait_ge(sems["f6_gelu"], geluw)
                            last = None
                            for half in range(2):
                                hv = xnb_views[xb[half][0]]
                                for dc in range(DC):
                                    last = te.matmul(
                                        psH[gh % 2][:, half * 512:(half + 1) * 512],
                                        w1sl[(gsl_base + j // 2) % 2][:].rearrange(
                                            "p (dc f) -> p dc f", dc=DC)[
                                            :, dc, (j % 2) * 128:(j % 2 + 1) * 128],
                                        hv[:, dc, :],
                                        start=(dc == 0), stop=(dc == DC - 1))
                            last.then_inc(sems["f6_h"], 1)
                        ph.add("tensor", _hmm)
                        h_v = inc("f6_h", 1)

                        def _gelu(sc, j=j, fc=fc, gh=gh, wv=h_v):
                            sc.wait_ge(sems["f6_h"], wv)
                            sc.activation(hTg[:, j * 1024:(j + 1) * 1024],
                                          psH[gh % 2][:], AF.Gelu,
                                          bias=b1_sb[:, fc:fc + 1]
                                          ).then_inc(sems["f6_gelu"], 1)
                        ph.add("scalar", _gelu)
                        inc("f6_gelu", 1)

                    gelu_all = cnt["f6_gelu"]

                    for tt in range(8):
                        for dj in range(4):
                            gy = cnt["f6_y"]

                            def _ymm(te, tt=tt, dj=dj, gy=gy, gw=gelu_all,
                                     w2v=w2_v, yevw=gy - 1,
                                     first=(tt == 0 and dj == 0)):
                                if first:
                                    te.wait_ge(sems["f6_gelu"], gw)
                                    te.wait_ge(sems["f6_w2"], w2v)
                                if gy >= 2:
                                    te.wait_ge(sems["f6_yev"], yevw)
                                last = None
                                for j in range(GC):
                                    last = te.matmul(
                                        psY[gy % 2][:],
                                        hTg[:, j * 1024 + tt * 128:
                                            j * 1024 + (tt + 1) * 128],
                                        w2gb[:, j * D + dj * 512:j * D + (dj + 1) * 512],
                                        start=(j == 0), stop=(j == GC - 1))
                                last.then_inc(sems["f6_y"], 1)
                            ph.add("tensor", _ymm)
                            y_v = inc("f6_y", 1)

                            def _yev(ve, g=g, tt=tt, dj=dj, gy=gy, wv=y_v,
                                     siluw=8 * blk):
                                ve.wait_ge(sems["f6_y"], wv)
                                ya = y_acc[:, tt * D + dj * 512:tt * D + (dj + 1) * 512]
                                if g == 0:
                                    if siluw >= 1 and tt == 0 and dj == 0:
                                        ve.wait_ge(sems["f6_silu"], siluw)
                                    ve.tensor_tensor(ya, psY[gy % 2][:],
                                                     b2b[:, dj * 512:(dj + 1) * 512],
                                                     ALU.add).then_inc(sems["f6_yev"], 1)
                                else:
                                    ve.tensor_tensor(ya, ya, psY[gy % 2][:], ALU.add
                                                     ).then_inc(sems["f6_yev"], 1)
                            ph.add("vector", _yev)
                            inc("f6_yev", 1)

                        if g == NG - 1:
                            gtt = blk * 8 + tt

                            def _sxld(sy, b=b, t0=t0, tt=tt, gtt=gtt,
                                      ow=16 * (gtt - 1)):
                                if gtt >= 2:
                                    sy.wait_ge(sems["f6_out"], ow)
                                r0 = t0 + tt * 128
                                sy.dma_start(out=xld[gtt % 2][:],
                                             in_=x_in[b, r0:r0 + 128, :]
                                             ).then_inc(sems["f6_xld"], 16)
                            ph.add("sync", _sxld)
                            xld_v = inc("f6_xld", 16)

                            def _sldf(sc, b=b, t0=t0, tt=tt, gtt=gtt,
                                      ow=16 * (gtt - 1),
                                      s4w=s4_after_b if b == 1 else s4_after_b // 2,
                                      first=(tt == 0)):
                                if first:
                                    sc.wait_ge(sems["s4_st"], s4w)
                                if gtt >= 2:
                                    sc.wait_ge(sems["f6_out"], ow)
                                r0 = t0 + tt * 128
                                sc.dma_start(out=sld[gtt % 2][:],
                                             in_=s_s[b, r0:r0 + 128, :]
                                             ).then_inc(sems["f6_sld"], 16)
                            ph.add("scalar", _sldf)
                            sld_v = inc("f6_sld", 16)

                            yev_all = cnt["f6_yev"]

                            def _silu(sc, tt=tt, wv=yev_all, cw=cnt["f6_cmb"]):
                                sc.wait_ge(sems["f6_yev"], wv)
                                if cw >= 1:
                                    sc.wait_ge(sems["f6_cmb"], cw)
                                sc.activation(gg[:], y_acc[:, tt * D:(tt + 1) * D],
                                              AF.Silu).then_inc(sems["f6_silu"], 1)
                            ph.add("scalar", _silu)
                            silu_v = inc("f6_silu", 1)

                            def _cmb(ve, gtt=gtt, wv=silu_v, sv=sld_v, xv=xld_v,
                                     ow=16 * (gtt - 1)):
                                ve.wait_ge(sems["f6_silu"], wv)
                                ve.wait_ge(sems["f6_sld"], sv)
                                ve.wait_ge(sems["f6_xld"], xv)
                                if gtt >= 2:
                                    ve.wait_ge(sems["f6_out"], ow)
                                ve.tensor_tensor(gg[:], gg[:], sld[gtt % 2][:],
                                                 ALU.mult)
                                ve.tensor_tensor(ost[gtt % 2][:], gg[:],
                                                 xld[gtt % 2][:], ALU.add
                                                 ).then_inc(sems["f6_cmb"], 1)
                            ph.add("vector", _cmb)
                            cmb_v = inc("f6_cmb", 1)

                            def _ost(sy, b=b, t0=t0, tt=tt, gtt=gtt, wv=cmb_v):
                                sy.wait_ge(sems["f6_cmb"], wv)
                                r0 = t0 + tt * 128
                                sy.dma_start(out=out[b, r0:r0 + 128, :],
                                             in_=ost[gtt % 2][:]
                                             ).then_inc(sems["f6_out"], 16)
                            ph.add("sync", _ost)
                            inc("f6_out", 16)

                barrier(ph)
                ph.flush(block)

            ph = Phase()
            out_total = cnt["f6_out"]

            def _fin(sy, wv=out_total):
                sy.wait_ge(sems["f6_out"], wv)
            ph.add("sync", _fin)
            ph.flush(block)

    return nc


_CACHED = {}


def kernel(L0_attn_x, L0_attn_w_qkv, L0_attn_w_gates, L0_attn_w_out,
           L0_ffn_w1, L0_ffn_b1, L0_ffn_w2, L0_ffn_b2):
    x = np.ascontiguousarray(np.asarray(L0_attn_x, dtype=np.float32))
    wqkv_a = np.ascontiguousarray(np.asarray(L0_attn_w_qkv, dtype=np.float32))
    wg_a = np.ascontiguousarray(np.asarray(L0_attn_w_gates, dtype=np.float32))
    wout_a = np.ascontiguousarray(np.asarray(L0_attn_w_out, dtype=np.float32))
    w1_a = np.ascontiguousarray(np.asarray(L0_ffn_w1, dtype=np.float32))
    b1_a = np.ascontiguousarray(np.asarray(L0_ffn_b1, dtype=np.float32))
    w2_a = np.ascontiguousarray(np.asarray(L0_ffn_w2, dtype=np.float32))
    b2_a = np.ascontiguousarray(np.asarray(L0_ffn_b2, dtype=np.float32))

    if "nc" not in _CACHED:
        _CACHED["nc"] = build()
    nc = _CACHED["nc"]

    in_maps = [{"x": x[c * B:(c + 1) * B], "wqkv": wqkv_a, "wg": wg_a,
                "wout": wout_a, "w1": w1_a, "b1": b1_a, "w2": w2_a, "b2": b2_a}
               for c in range(NCORES)]
    res = run_bass_kernel_spmd(nc, in_maps, core_ids=list(range(NCORES)))
    return np.concatenate([r["out"] for r in res.results], axis=0)
